# version 01
# baseline (speedup 1.0000x reference)
import numpy as np
from contextlib import ExitStack

import concourse.bass as bass
import concourse.mybir as mybir
from concourse.bass_utils import run_bass_kernel_spmd

F32 = mybir.dt.float32
Alu = mybir.AluOpType
Act = mybir.ActivationFunctionType
AxX = mybir.AxisListType.X

DM, DI, N, FF, TAPS = 1024, 2048, 16, 4096, 4
Tc = 256
KDM = DM // 128
MDI = DI // 128
MIP = 2 * DI // 128
MFF = FF // 128
MDM = DM // 128
G = 16

LN_EPS = 1e-5
FFN_EPS = 1e-5
SCAN_EPS = 1e-6
DT_MIN = 1e-4


class Pool:

    def __init__(self, n):
        self.tok = [None] * n
        self.i = 0
        self.n = n

    def acquire(self):
        s = self.i % self.n
        self.i += 1
        t = self.tok[s]
        self.tok[s] = None
        return s, t

    def release(self, s, sem, val):
        self.tok[s] = (sem, val)


class Sched:
    def __init__(self):
        self.ops = {e: [] for e in ("tensor", "vector", "scalar", "sync", "gpsimd")}
        self.cnt = {}

    def op(self, eng, fn, waits=(), inc=None, inc_by=1):
        w = []
        for t in waits:
            if t is None:
                continue
            s, v = t
            assert v <= self.cnt.get(id(s), 0), f"wait on future event ({eng})"
            if v > 0:
                w.append((s, v))
        val = None
        if inc is not None:
            self.cnt[id(inc)] = self.cnt.get(id(inc), 0) + inc_by
            val = self.cnt[id(inc)]
        self.ops[eng].append((fn, w, inc, inc_by))
        return val


def build_program(S=4096, debug=False):
    NCH = S // Tc
    dbg = {}
    nc = bass.Bass("TRN2", target_bir_lowering=False, debug=False)

    x_d = nc.dram_tensor("x", [S, DM], F32, kind="ExternalInput").ap()
    w_ip = nc.dram_tensor("in_proj_w", [2 * DI, DM], F32, kind="ExternalInput").ap()
    w_cv = nc.dram_tensor("conv_w", [DI, DI, TAPS], F32, kind="ExternalInput").ap()
    b_cv = nc.dram_tensor("conv_b", [DI], F32, kind="ExternalInput").ap()
    w_dt = nc.dram_tensor("dt_w", [DI, DI], F32, kind="ExternalInput").ap()
    b_dt = nc.dram_tensor("dt_b", [DI], F32, kind="ExternalInput").ap()
    a_log = nc.dram_tensor("A_log", [DI, N], F32, kind="ExternalInput").ap()
    d_v = nc.dram_tensor("Dv", [DI], F32, kind="ExternalInput").ap()
    w_b = nc.dram_tensor("B_w", [N, DI], F32, kind="ExternalInput").ap()
    w_c = nc.dram_tensor("C_w", [N, DI], F32, kind="ExternalInput").ap()
    w_ow = nc.dram_tensor("out_w", [DM, DI], F32, kind="ExternalInput").ap()
    w_f1 = nc.dram_tensor("W1", [FF, DM], F32, kind="ExternalInput").ap()
    b_f1 = nc.dram_tensor("b1", [FF], F32, kind="ExternalInput").ap()
    w_f2 = nc.dram_tensor("W2", [DM, FF], F32, kind="ExternalInput").ap()
    b_f2 = nc.dram_tensor("b2", [DM], F32, kind="ExternalInput").ap()
    out_d = nc.dram_tensor("out", [S, DM], F32, kind="ExternalOutput").ap()

    ipT = nc.dram_tensor("ipT", [MIP * KDM * 128, 128], F32).ap()
    cvT = nc.dram_tensor("cvT", [MDI * TAPS * MDI * 128, 128], F32).ap()
    dtT = nc.dram_tensor("dtT", [MDI * MDI * 128, 128], F32).ap()
    owT = nc.dram_tensor("owT", [MDM * MDI * 128, 128], F32).ap()
    f1T = nc.dram_tensor("f1T", [MFF * KDM * 128, 128], F32).ap()
    f2T = nc.dram_tensor("f2T", [MDM * MFF * 128, 128], F32).ap()

    pos = [0]

    def seg(w):
        c = pos[0]
        pos[0] += w
        return c

    A_SX = seg(2 * 2 * DM)
    A_XNT = seg(2 * DM)
    A_OUT = seg(2 * DM)
    A_XNF = seg(KDM * Tc)
    XCW = Tc + 3
    A_XC = seg(MDI * XCW)
    A_Z = seg(MDI * Tc)
    A_U = seg(MDI * Tc)
    A_DT = seg(MDI * Tc)
    A_DTU = seg(MDI * Tc)
    A_Y = seg(MDI * Tc)
    A_AT = seg(MDM * Tc)
    A_WR = seg(2 * G * 128)
    A_DA = seg(2 * Tc)
    A_B = seg(Tc)
    A_SS = seg(Tc)
    A_P = seg(Tc)
    A_ST = seg(Tc)
    A_ASB = seg(Tc)
    A_BCW = seg(2 * MDI * N)
    A_SBC = seg(2 * Tc)
    A_CB = seg(MDI)
    A_DB = seg(MDI)
    A_DV = seg(MDI)
    A_B1 = seg(MFF)
    A_B2 = seg(MDM)
    A_ID = seg(128)
    A_ONE = seg(1)
    A_ONR = seg(128)
    A_STAT = seg(16)
    A_GAM = seg(8)
    A_SEL = seg(N * 128)
    ACOLS = ((pos[0] + 127) // 128) * 128
    assert ACOLS <= 47500, f"arena too big: {ACOLS}"

    P_NAT = A_Z
    P_STR = A_SX
    P_G1 = A_DT
    P_G2 = A_DTU

    sch = Sched()

    with ExitStack() as est:
        en = est.enter_context
        ar_h = en(nc.sbuf_tensor("arena", [128, ACOLS], F32))
        ps_h = en(nc.psum_tensor("ps", [128, 4096], F32))
        sDMA = en(nc.semaphore("sDMA"))
        sST = en(nc.semaphore("sST"))
        sT = en(nc.semaphore("sT"))
        sS = en(nc.semaphore("sS"))
        sV = en(nc.semaphore("sV"))
        sG = en(nc.semaphore("sG"))
        block = en(nc.Block())
        sTAP = en(nc.semaphore("sTAP"))

        def a(c0, w, p0=0, pn=128):
            return ar_h[p0:p0 + pn, c0:c0 + w]

        def bank(k, c0=0, w=512, p0=0, pn=128):
            return ps_h[p0:p0 + pn, k * 512 + c0:k * 512 + c0 + w]

        TPB, PPB, REPB, BPFB = (0, 1), (4, 5), (2, 3), (6, 7)
        tp_pool, pp_pool, rep_pool, bpf_pool = Pool(2), Pool(2), Pool(2), Pool(2)
        da_pool = Pool(2)

        op = sch.op

        def tap(name, c0, w, after, pn=128):
            if not debug:
                return None
            t = nc.dram_tensor("dbg_" + name, [pn, w], F32,
                               kind="ExternalOutput").ap()
            dbg[name] = t
            return op("gpsimd", lambda e, t=t, c0=c0, w=w, pn=pn: e.dma_start(
                out=t, in_=a(c0, w, pn=pn)), waits=[after], inc=sTAP, inc_by=16)

        g_iota = op("gpsimd", lambda e: e.iota(
            out=a(A_ID, 128), pattern=[[-1, 128]], base=0, channel_multiplier=1,
            allow_small_or_imprecise_dtypes=True), inc=sG)
        v_id = op("vector", lambda e: e.tensor_scalar(
            out=a(A_ID, 128), in0=a(A_ID, 128), scalar1=0.0, scalar2=None,
            op0=Alu.is_equal), waits=[(sG, g_iota)], inc=sV)
        g_sel = op("gpsimd", lambda e: e.iota(
            out=a(A_SEL, N * 128, pn=N).rearrange("p (n j) -> p n j", j=128),
            pattern=[[-1, N], [0, 128]], base=0, channel_multiplier=1,
            allow_small_or_imprecise_dtypes=True), inc=sG)
        op("vector", lambda e: e.tensor_scalar(
            out=a(A_SEL, N * 128, pn=N), in0=a(A_SEL, N * 128, pn=N),
            scalar1=0.0, scalar2=None, op0=Alu.is_equal),
            waits=[(sG, g_sel)], inc=sV)
        op("vector", lambda e: e.memset(a(A_ONE, 1), 1.0), inc=sV)
        v_onr = op("vector", lambda e: e.memset(a(A_ONR, 128, pn=1), 1.0), inc=sV)
        op("vector", lambda e: e.memset(a(A_XC, MDI * XCW), 0.0), inc=sV)
        op("vector", lambda e: e.memset(a(A_ST, Tc), 0.0), inc=sV)

        d_last = op("sync", lambda e: e.dma_start(
            out=a(A_CB, MDI), in_=b_cv.rearrange("(c p) -> p c", p=128),
            allow_slow_non_contiguous=True),
            inc=sDMA, inc_by=16)
        d_last = op("sync", lambda e: e.dma_start(
            out=a(A_DB, MDI), in_=b_dt.rearrange("(c p) -> p c", p=128),
            allow_slow_non_contiguous=True),
            inc=sDMA, inc_by=16)
        d_last = op("sync", lambda e: e.dma_start(
            out=a(A_DV, MDI), in_=d_v.rearrange("(c p) -> p c", p=128),
            allow_slow_non_contiguous=True),
            inc=sDMA, inc_by=16)
        d_last = op("sync", lambda e: e.dma_start(
            out=a(A_B1, MFF), in_=b_f1.rearrange("(c p) -> p c", p=128),
            allow_slow_non_contiguous=True),
            inc=sDMA, inc_by=16)
        d_last = op("sync", lambda e: e.dma_start(
            out=a(A_B2, MDM), in_=b_f2.rearrange("(c p) -> p c", p=128),
            allow_slow_non_contiguous=True),
            inc=sDMA, inc_by=16)
        d_alog = op("sync", lambda e: e.dma_start(
            out=a(A_ASB, Tc).rearrange("p (c n) -> p c n", n=N),
            in_=a_log.rearrange("(c p) n -> p c n", p=128)), inc=sDMA, inc_by=16)
        s_exp = op("scalar", lambda e: e.activation(
            out=a(A_ASB, Tc), in_=a(A_ASB, Tc), func=Act.Exp),
            waits=[(sDMA, d_alog)], inc=sS)
        v_asb = op("vector", lambda e: e.tensor_scalar(
            out=a(A_ASB, Tc), in0=a(A_ASB, Tc), scalar1=-1.0, scalar2=None,
            op0=Alu.mult), waits=[(sS, s_exp)], inc=sV)
        tap("asb", A_ASB, Tc, (sV, v_asb))

        nat_tok = [None]
        str_tok = [None]
        qtm_tok = [None]
        n_prep_stores = [0]

        for wi, wsrc in enumerate((w_b, w_c)):
            d0 = op("sync", lambda e, w=wsrc: e.dma_start(
                out=a(P_NAT, DI, pn=N), in_=w), waits=[nat_tok[0]],
                inc=sDMA, inc_by=16)
            for half in range(2):
                slot, tok = tp_pool.acquire()
                last = None
                for j in range(8):
                    k = half * 8 + j
                    last = op("tensor", lambda e, k=k, j=j, slot=slot: e.matmul(
                        bank(TPB[slot], j * N, N),
                        a(P_NAT + k * 128, 128, pn=N),
                        a(A_ID, N, pn=N),
                        start=(j == 0), stop=(j == 7), is_transpose=True),
                        waits=([(sDMA, d0), (sV, v_id), tok] if j == 0 else []),
                        inc=(sT if j == 7 else None))
                ev = op("scalar", lambda e, slot=slot, wi=wi, half=half: e.activation(
                    out=a(A_BCW + wi * MDI * N + half * 8 * N, 8 * N),
                    in_=bank(TPB[slot], 0, 8 * N), func=Act.Copy),
                    waits=[(sT, last)], inc=sS)
                tp_pool.release(slot, sS, ev)
            nat_tok[0] = (sT, last)

        def prep_weight(nat_src_fn, nblocks, ktiles, scratch, width,
                        quant=None, strided_taps=False):
            for m in range(nblocks):
                d0 = op("sync", lambda e, m=m: e.dma_start(
                    out=a(P_NAT, width), in_=nat_src_fn(m)),
                    waits=[nat_tok[0]], inc=sDMA, inc_by=16)
                src_base = P_NAT
                first_w = [(sDMA, d0)]
                if quant is not None:
                    src_base, vq = quant(m, d0)
                    op("tensor", None, waits=[(sV, vq)])
                    first_w = []
                    nat_tok[0] = (sV, vq)
                    qtm_tok[0] = None
                last_ev = None
                nbatch = (ktiles + 3) // 4
                for bt in range(nbatch):
                    slot, tok = tp_pool.acquire()
                    last = None
                    nin = min(4, ktiles - bt * 4)
                    for j in range(nin):
                        k = bt * 4 + j
                        if strided_taps:
                            tap, i = k % TAPS, k // TAPS
                            src = a(src_base, DI * TAPS).rearrange(
                                "p (i t) -> p i t", t=TAPS)[
                                :, i * 128:(i + 1) * 128, tap]
                        else:
                            src = a(src_base + k * 128, 128)
                        w0 = [tok] + (first_w if (bt == 0 and j == 0) else [])
                        last = op("tensor",
                                  lambda e, src=src, slot=slot, j=j, nin=nin: e.matmul(
                                      bank(TPB[slot], j * 128, 128), src,
                                      a(A_ID, 128), start=(j == 0),
                                      stop=(j == nin - 1), is_transpose=True),
                                  waits=(w0 if j == 0 else []),
                                  inc=(sT if j == nin - 1 else None))
                    ev = op("scalar", lambda e, slot=slot, bt=bt, nin=nin: e.activation(
                        out=a(P_STR + bt * 512, nin * 128),
                        in_=bank(TPB[slot], 0, nin * 128), func=Act.Copy),
                        waits=[(sT, last)] + ([str_tok[0]] if bt == 0 else []),
                        inc=sS)
                    tp_pool.release(slot, sS, ev)
                    last_ev = ev
                if quant is None:
                    nat_tok[0] = (sT, last)
                else:
                    qtm_tok[0] = (sT, last)
                stv = op("gpsimd",
                         lambda e, m=m, kt=ktiles, scr=scratch: e.dma_start(
                             out=scr[m * kt * 128:(m + 1) * kt * 128, :].rearrange(
                                 "(t p) c -> p t c", p=128),
                             in_=a(P_STR, kt * 128).rearrange(
                                 "p (t c) -> p t c", c=128)),
                         waits=[(sS, last_ev)], inc=sST, inc_by=16)
                str_tok[0] = (sST, stv)
                n_prep_stores[0] += 1

        def prep_conv():
            for m in range(MDI):
                d0 = op("sync", lambda e, m=m: e.dma_start(
                    out=a(P_NAT, DI * TAPS),
                    in_=w_cv[m * 128:(m + 1) * 128, :, :].rearrange(
                        "p i t -> p (i t)")),
                    waits=[nat_tok[0]], inc=sDMA, inc_by=16)
                last_ev = None
                last = None
                for bt in range(TAPS * MDI // 4):
                    slot, tok = tp_pool.acquire()
                    for j in range(4):
                        k = bt * 4 + j
                        tap, i = k // MDI, k % MDI
                        src = a(P_NAT, DI * TAPS).rearrange(
                            "p (i t) -> p i t", t=TAPS)[
                            :, i * 128:(i + 1) * 128, tap]
                        w0 = [tok] + ([(sDMA, d0)] if (bt == 0 and j == 0) else [])
                        last = op("tensor",
                                  lambda e, src=src, slot=slot, j=j: e.matmul(
                                      bank(TPB[slot], j * 128, 128), src,
                                      a(A_ID, 128), start=(j == 0),
                                      stop=(j == 3), is_transpose=True),
                                  waits=(w0 if j == 0 else []),
                                  inc=(sT if j == 3 else None))
                    ev = op("scalar", lambda e, slot=slot, bt=bt: e.activation(
                        out=a(P_STR + bt * 512, 512),
                        in_=bank(TPB[slot], 0, 512), func=Act.Copy),
                        waits=[(sT, last)] + ([str_tok[0]] if bt == 0 else []),
                        inc=sS)
                    tp_pool.release(slot, sS, ev)
                    last_ev = ev
                nat_tok[0] = (sT, last)
                kt = TAPS * MDI
                stv = op("gpsimd", lambda e, m=m, kt=kt: e.dma_start(
                    out=cvT[m * kt * 128:(m + 1) * kt * 128, :].rearrange(
                        "(t p) c -> p t c", p=128),
                    in_=a(P_STR, kt * 128).rearrange("p (t c) -> p t c", c=128)),
                    waits=[(sS, last_ev)], inc=sST, inc_by=16)
                str_tok[0] = (sST, stv)
                n_prep_stores[0] += 1

        prep_weight(lambda m: w_ip[m * 128:(m + 1) * 128, :], MIP, KDM, ipT, DM)
        prep_conv()
        prep_weight(lambda m: w_dt[m * 128:(m + 1) * 128, :], MDI, MDI, dtT, DI)
        prep_weight(lambda m: w_ow[m * 128:(m + 1) * 128, :], MDM, MDI, owT, DI)

        def gamma_pass(wsrc, nblocks, width, gcol):
            op("vector", lambda e: e.memset(a(gcol, 1), 0.0), inc=sV)
            rv = None
            for m in range(nblocks):
                d0 = op("sync", lambda e, m=m, w=wsrc: e.dma_start(
                    out=a(P_NAT, width), in_=w[m * 128:(m + 1) * 128, :]),
                    waits=[nat_tok[0]], inc=sDMA, inc_by=16)
                gv = op("vector", lambda e, width=width: e.reduce_sum(
                    out=a(gcol + 1, 1), in_=a(P_NAT, width), axis=AxX,
                    apply_absolute_value=True), waits=[(sDMA, d0)], inc=sV)
                rv = op("vector", lambda e: e.tensor_add(
                    out=a(gcol, 1), in0=a(gcol, 1), in1=a(gcol + 1, 1)),
                    waits=[(sV, gv)], inc=sV)
                nat_tok[0] = (sV, rv)
            mm = op("tensor", lambda e: e.matmul(
                bank(PPB[0], 0, 1, pn=1), a(gcol, 1), a(A_ONE, 1),
                start=True, stop=True), waits=[(sV, rv)], inc=sT)
            e1 = op("scalar", lambda e: e.activation(
                out=a(gcol + 2, 1, pn=1), in_=bank(PPB[0], 0, 1, pn=1),
                func=Act.Copy), waits=[(sT, mm)], inc=sS)
            nelem = float(nblocks * 128 * width)
            w1_ = op("vector", lambda e, nelem=nelem: e.tensor_scalar(
                out=a(gcol + 3, 1, pn=1), in0=a(gcol + 2, 1, pn=1),
                scalar1=1.0 / nelem, scalar2=FFN_EPS, op0=Alu.mult, op1=Alu.add),
                waits=[(sS, e1)], inc=sV)
            w2_ = op("vector", lambda e: e.reciprocal(
                out=a(gcol + 4, 1, pn=1), in_=a(gcol + 3, 1, pn=1)),
                waits=[(sV, w1_)], inc=sV)
            w3_ = op("vector", lambda e: e.tensor_mul(
                out=a(gcol + 5, 1, pn=1), in0=a(gcol + 3, 1, pn=1),
                in1=a(gcol + 4, 1, pn=1)), waits=[(sV, w2_)], inc=sV)
            w4_ = op("vector", lambda e: e.tensor_scalar(
                out=a(gcol + 5, 1, pn=1), in0=a(gcol + 5, 1, pn=1),
                scalar1=-1.0, scalar2=2.0, op0=Alu.mult, op1=Alu.add),
                waits=[(sV, w3_)], inc=sV)
            v2 = op("vector", lambda e: e.tensor_mul(
                out=a(gcol + 4, 1, pn=1), in0=a(gcol + 4, 1, pn=1),
                in1=a(gcol + 5, 1, pn=1)), waits=[(sV, w4_)], inc=sV)
            mm2 = op("tensor", lambda e: e.matmul(
                bank(PPB[1], 0, 1), a(A_ONR, 128, pn=1), a(gcol + 4, 1, pn=1),
                start=True, stop=True), waits=[(sV, v2), (sV, v_onr)], inc=sT)
            e2 = op("scalar", lambda e: e.activation(
                out=a(gcol + 6, 1), in_=bank(PPB[1], 0, 1), func=Act.Copy),
                waits=[(sT, mm2)], inc=sS)
            return gcol + 6, e2

        r1c, r1ev = gamma_pass(w_f1, MFF, DM, A_GAM)

        def quant_w1(m, d0):
            op("vector", lambda e: e.tensor_scalar(
                out=a(P_G1, DM), in0=a(P_NAT, DM), scalar1=a(r1c, 1),
                scalar2=0.5, op0=Alu.mult, op1=Alu.is_gt),
                waits=[(sDMA, d0), (sS, r1ev), qtm_tok[0]], inc=sV)
            op("vector", lambda e: e.tensor_scalar(
                out=a(P_G2, DM), in0=a(P_NAT, DM), scalar1=a(r1c, 1),
                scalar2=-0.5, op0=Alu.mult, op1=Alu.is_lt), inc=sV)
            vq = op("vector", lambda e: e.tensor_sub(
                out=a(P_G1, DM), in0=a(P_G1, DM), in1=a(P_G2, DM)), inc=sV)
            return P_G1, vq

        prep_weight(lambda m: w_f1[m * 128:(m + 1) * 128, :], MFF, KDM, f1T, DM,
                    quant=quant_w1)

        r2c, r2ev = gamma_pass(w_f2, MDM, FF, A_GAM)

        def quant_w2(m, d0):
            op("vector", lambda e: e.tensor_scalar(
                out=a(P_STR + 4096, FF), in0=a(P_NAT, FF), scalar1=a(r2c, 1),
                scalar2=0.5, op0=Alu.mult, op1=Alu.is_gt),
                waits=[(sDMA, d0), (sS, r2ev), qtm_tok[0]], inc=sV)
            op("vector", lambda e: e.tensor_scalar(
                out=a(P_NAT + 4096, FF), in0=a(P_NAT, FF), scalar1=a(r2c, 1),
                scalar2=-0.5, op0=Alu.mult, op1=Alu.is_lt), inc=sV)
            vq = op("vector", lambda e: e.tensor_sub(
                out=a(P_STR + 4096, FF), in0=a(P_STR + 4096, FF),
                in1=a(P_NAT + 4096, FF)), inc=sV)
            return P_STR + 4096, vq

        prep_weight(lambda m: w_f2[m * 128:(m + 1) * 128, :], MDM, MFF, f2T, FF,
                    quant=quant_w2)

        PREP_STORES = n_prep_stores[0]
        barrier = (sST, 16 * PREP_STORES)

        conv_last, ffn1_last, ffn2_last = {}, {}, {}
        out_mm_last, ft_last, fev_last, ym_ev, a_ev = {}, {}, {}, {}, {}
        xload, store_val = {}, {}
        wgrp = [0]
        ring_last = {}

        def wload(scratch, g0, ntiles=G, extra=()):
            gi = wgrp[0]
            wgrp[0] += 1
            slot = gi % 2
            waits = list(extra)
            if gi >= 2:
                waits.append((sT, ring_last[gi - 2]))
            dv = op("sync",
                    lambda e, scr=scratch, g0=g0, nt=ntiles, slot=slot: e.dma_start(
                        out=a(A_WR + slot * G * 128, nt * 128).rearrange(
                            "p (t c) -> p t c", c=128),
                        in_=scr[g0 * 128:(g0 + nt) * 128, :].rearrange(
                            "(t p) c -> p t c", p=128)),
                    waits=waits, inc=sDMA, inc_by=16)
            return gi, slot, dv

        def wtile(slot, i):
            return a(A_WR + slot * G * 128 + i * 128, 128)

        for c in range(min(2, NCH)):
            xload[c] = op("sync", lambda e, c=c: e.dma_start(
                out=a(A_SX + (c % 2) * 2 * DM, 2 * DM).rearrange(
                    "p (t d) -> p t d", d=DM),
                in_=x_d[c * Tc:(c + 1) * Tc, :].rearrange(
                    "(t p) d -> p t d", p=128)),
                waits=([barrier] if c == 0 else []), inc=sDMA, inc_by=16)

        for c in range(NCH):
            sx = A_SX + (c % 2) * 2 * DM

            hw = [(sDMA, xload[c])]
            if c > 0:
                hw.append((sT, ffn2_last[c - 1]))
            op("vector", lambda e: e.tensor_copy(
                out=a(A_XC, MDI * XCW).rearrange(
                    "p (m w) -> p m w", w=XCW)[:, :, 0:3],
                in_=a(A_XC, MDI * XCW).rearrange(
                    "p (m w) -> p m w", w=XCW)[:, :, Tc:Tc + 3]),
                waits=hw, inc=sV)
            vv = None
            for tt in range(2):
                mv = op("vector", lambda e, tt=tt, sx=sx: e.reduce_sum(
                    out=a(A_STAT + 6 + tt, 1), in_=a(sx + tt * DM, DM), axis=AxX),
                    inc=sV)
                nv = op("vector", lambda e, tt=tt: e.tensor_scalar(
                    out=a(A_STAT + 8 + tt, 1), in0=a(A_STAT + 6 + tt, 1),
                    scalar1=-1.0 / DM, scalar2=None, op0=Alu.mult),
                    waits=[(sV, mv)], inc=sV)
                op("vector", lambda e, tt=tt, sx=sx: e.tensor_scalar(
                    out=a(A_XNT + tt * DM, DM), in0=a(sx + tt * DM, DM),
                    scalar1=a(A_STAT + 8 + tt, 1), scalar2=None, op0=Alu.add),
                    waits=[(sV, nv)], inc=sV)
                op("vector", lambda e, tt=tt: e.tensor_mul(
                    out=a(A_DTU + tt * DM, DM), in0=a(A_XNT + tt * DM, DM),
                    in1=a(A_XNT + tt * DM, DM)), inc=sV)
                qv = op("vector", lambda e, tt=tt: e.reduce_sum(
                    out=a(A_STAT + 10 + tt, 1), in_=a(A_DTU + tt * DM, DM),
                    axis=AxX), inc=sV)
                vv = op("vector", lambda e, tt=tt: e.tensor_scalar(
                    out=a(A_STAT + tt, 1), in0=a(A_STAT + 10 + tt, 1),
                    scalar1=1.0 / DM, scalar2=LN_EPS, op0=Alu.mult, op1=Alu.add),
                    waits=[(sV, qv)], inc=sV)
            sq = op("scalar", lambda e: e.activation(
                out=a(A_STAT + 2, 2), in_=a(A_STAT, 2), func=Act.Sqrt),
                waits=[(sV, vv)], inc=sS)
            rcv = op("vector", lambda e: e.reciprocal(
                out=a(A_STAT + 4, 2), in_=a(A_STAT + 2, 2)), waits=[(sS, sq)],
                inc=sV)
            if c == 0:
                tap("stat", A_STAT, 16, (sV, rcv))
            vxn = None
            for tt in range(2):
                vxn = op("vector", lambda e, tt=tt: e.tensor_scalar(
                    out=a(A_XNT + tt * DM, DM), in0=a(A_XNT + tt * DM, DM),
                    scalar1=a(A_STAT + 4 + tt, 1), scalar2=None, op0=Alu.mult),
                    waits=([(sV, rcv)] if tt == 0 else []), inc=sV)
            a_ev[c] = vxn
            if c == 0:
                tap("xnt", A_XNT, 2 * DM, (sV, a_ev[0]))

            xnf_ready = None
            for bt in range(4):
                tt, dq = bt // 2, bt % 2
                slot, tok = tp_pool.acquire()
                last = None
                for j in range(4):
                    dc = dq * 4 + j
                    last = op("tensor",
                              lambda e, tt=tt, dc=dc, j=j, slot=slot: e.matmul(
                                  bank(TPB[slot], j * 128, 128),
                                  a(A_XNT + tt * DM + dc * 128, 128), a(A_ID, 128),
                                  start=(j == 0), stop=(j == 3), is_transpose=True),
                              waits=([(sV, a_ev[c]), tok] if j == 0 else []),
                              inc=(sT if j == 3 else None))
                ew = [(sT, last)]
                if bt == 0 and c > 0:
                    ew.append((sT, ffn1_last[c - 1]))
                xnf_ready = op("scalar",
                               lambda e, tt=tt, dq=dq, slot=slot: e.activation(
                                   out=a(A_XNF, KDM * Tc).rearrange(
                                       "p (dc t) -> p dc t", t=Tc)[
                                       :, dq * 4:dq * 4 + 4,
                                       tt * 128:(tt + 1) * 128],
                                   in_=bank(TPB[slot], 0, 512).rearrange(
                                       "p (j t) -> p j t", t=128),
                                   func=Act.Copy), waits=ew, inc=sS)
                tp_pool.release(slot, sS, xnf_ready)
            if c == 0:
                tap("xnf", A_XNF, KDM * Tc, (sS, xnf_ready))

            z_evac_all = None
            xc_evac_all = None
            for m in range(MIP):
                if m % 2 == 0:
                    gi, slot, dv = wload(
                        ipT, m * KDM,
                        extra=([barrier] if (c == 0 and m == 0) else ()))
                pslot, ptok = pp_pool.acquire()
                last = None
                for k in range(KDM):
                    w0 = []
                    if k == 0:
                        w0 = [(sDMA, dv), ptok, (sS, xnf_ready)]
                    last = op("tensor",
                              lambda e, m=m, k=k, slot=slot, pslot=pslot: e.matmul(
                                  bank(PPB[pslot], 0, Tc),
                                  wtile(slot, (m % 2) * KDM + k),
                                  a(A_XNF + k * Tc, Tc),
                                  start=(k == 0), stop=(k == KDM - 1)),
                              waits=w0, inc=(sT if k == KDM - 1 else None))
                if m % 2 == 1:
                    ring_last[gi] = last
                ew = [(sT, last)]
                if m == 0:
                    ew.append((sV, a_ev[c]))
                    if c > 0:
                        ew.append((sT, ffn2_last[c - 1]))
                if m < MDI:
                    ev = op("scalar", lambda e, m=m, pslot=pslot: e.activation(
                        out=a(A_XC + m * XCW + 3, Tc), in_=bank(PPB[pslot], 0, Tc),
                        func=Act.Copy), waits=ew, inc=sS)
                    xc_evac_all = ev
                else:
                    ev = op("scalar", lambda e, m=m, pslot=pslot: e.activation(
                        out=a(A_Z + (m - MDI) * Tc, Tc),
                        in_=bank(PPB[pslot], 0, Tc), func=Act.Silu),
                        waits=ew, inc=sS)
                    z_evac_all = ev
                pp_pool.release(pslot, sS, ev)

            if c == 0:
                tap("xc", A_XC, MDI * XCW, (sS, xc_evac_all))
                tap("z", A_Z, MDI * Tc, (sS, z_evac_all))

            u_evac_all = None
            for m in range(MDI):
                pslot, ptok = pp_pool.acquire()
                last = None
                for tg in range(TAPS):
                    gi, slot, dv = wload(cvT, (m * TAPS + tg) * MDI)
                    for i in range(MDI):
                        w0 = []
                        if i == 0:
                            w0 = [(sDMA, dv)]
                            if tg == 0:
                                w0.append(ptok)
                                if m == 0:
                                    w0.append((sS, xc_evac_all))
                        last = op("tensor",
                                  lambda e, m=m, tg=tg, i=i, slot=slot,
                                  pslot=pslot: e.matmul(
                                      bank(PPB[pslot], 0, Tc), wtile(slot, i),
                                      a(A_XC + i * XCW + tg, Tc),
                                      start=(tg == 0 and i == 0),
                                      stop=(tg == TAPS - 1 and i == MDI - 1)),
                                  waits=w0,
                                  inc=(sT if i == MDI - 1 else None))
                    ring_last[gi] = last
                conv_last[c] = last
                u_evac_all = op("scalar", lambda e, m=m, pslot=pslot: e.activation(
                    out=a(A_U + m * Tc, Tc), in_=bank(PPB[pslot], 0, Tc),
                    func=Act.Identity, bias=a(A_CB + m, 1)),
                    waits=[(sT, last)], inc=sS)
                pp_pool.release(pslot, sS, u_evac_all)

            u_tap = tap("u", A_U, MDI * Tc, (sS, u_evac_all)) if c == 0 else None

            dt_evac_all = None
            for m in range(MDI):
                gi, slot, dv = wload(dtT, m * MDI)
                pslot, ptok = pp_pool.acquire()
                last = None
                for k in range(MDI):
                    w0 = []
                    if k == 0:
                        w0 = [(sDMA, dv), ptok]
                        if m == 0:
                            w0.append((sS, u_evac_all))
                    last = op("tensor",
                              lambda e, k=k, slot=slot, pslot=pslot: e.matmul(
                                  bank(PPB[pslot], 0, Tc), wtile(slot, k),
                                  a(A_U + k * Tc, Tc),
                                  start=(k == 0), stop=(k == MDI - 1)),
                              waits=w0, inc=(sT if k == MDI - 1 else None))
                ring_last[gi] = last
                ew = [(sT, last)]
                if m == 0 and c > 0:
                    ew.append((sT, out_mm_last[c - 1]))
                dt_evac_all = op("scalar", lambda e, m=m, pslot=pslot: e.activation(
                    out=a(A_DT + m * Tc, Tc), in_=bank(PPB[pslot], 0, Tc),
                    func=Act.Sigmoid, bias=a(A_DB + m, 1)),
                    waits=ew, inc=sS)
                pp_pool.release(pslot, sS, dt_evac_all)

            bslot, btok = bpf_pool.acquire()
            last = None
            for half, base in ((0, 0), (1, MDI * N)):
                for k in range(MDI):
                    w0 = []
                    if half == 0 and k == 0:
                        w0 = [btok]
                    last = op("tensor",
                              lambda e, half=half, base=base, k=k,
                              bslot=bslot: e.matmul(
                                  bank(BPFB[bslot], half * Tc, Tc,
                                       p0=0, pn=N),
                                  a(A_BCW + base + k * N, N),
                                  a(A_U + k * Tc, Tc),
                                  start=(half == 0 and k == 0),
                                  stop=(half == 1 and k == MDI - 1)),
                              waits=w0, inc=(sT if (half == 1 and k == MDI - 1)
                                             else None))
            bc_evac = op("scalar", lambda e, bslot=bslot: e.activation(
                out=a(A_SBC, 2 * Tc, pn=N), in_=bank(BPFB[bslot], 0, 2 * Tc, pn=N),
                func=Act.Copy), waits=[(sT, last)], inc=sS)
            bpf_pool.release(bslot, sS, bc_evac)
            if c == 0:
                tap("sbc", A_SBC, 2 * Tc, (sS, bc_evac), pn=N)

            op("vector", lambda e: e.tensor_scalar(
                out=a(A_DT, MDI * Tc), in0=a(A_DT, MDI * Tc), scalar1=DT_MIN,
                scalar2=1.0, op0=Alu.max, op1=Alu.min),
                waits=[(sS, dt_evac_all)], inc=sV)
            dtp = op("vector", lambda e: e.tensor_mul(
                out=a(A_DTU, MDI * Tc), in0=a(A_DT, MDI * Tc),
                in1=a(A_U, MDI * Tc)), inc=sV)
            dt_tap = None
            if c == 0:
                dt_tap = tap("dt", A_DT, MDI * Tc, (sV, dtp))
                tap("dtu", A_DTU, MDI * Tc, (sV, dtp))

            scan_n_ev = {}
            for n in range(N):
                rslot, rtok = rep_pool.acquire()
                op("tensor", lambda e, n=n, rslot=rslot: e.matmul(
                    bank(REPB[rslot], 0, Tc), a(A_SEL + n * 128, 128, pn=N),
                    a(A_SBC, Tc, pn=N), start=True, stop=True),
                    waits=[rtok] + ([(sS, bc_evac)] if n == 0 else []))
                rep_n = op("tensor", lambda e, n=n, rslot=rslot: e.matmul(
                    bank(REPB[rslot], Tc, Tc), a(A_SEL + n * 128, 128, pn=N),
                    a(A_SBC + Tc, Tc, pn=N), start=False, stop=True),
                    inc=sT)
                last_v = None
                for cc in range(MDI):
                    dslot, dtok = da_pool.acquire()
                    dw = [dtok]
                    if n == 0 and cc == 0:
                        dw.append((sV, dtp))
                    da_val = op("scalar",
                                lambda e, n=n, cc=cc, dslot=dslot: e.activation(
                                    out=a(A_DA + dslot * Tc, Tc),
                                    in_=a(A_DT + cc * Tc, Tc), func=Act.Exp,
                                    scale=a(A_ASB + cc * N + n, 1)),
                                waits=dw, inc=sS)
                    bw = []
                    if cc == 0:
                        bw = [(sT, rep_n)]
                    op("vector", lambda e, cc=cc, rslot=rslot: e.tensor_tensor(
                        out=a(A_B, Tc), in0=a(A_DTU + cc * Tc, Tc),
                        in1=bank(REPB[rslot], 0, Tc), op=Alu.mult),
                        waits=bw, inc=None)
                    op("vector", lambda e: e.tensor_scalar(
                        out=a(A_B, Tc), in0=a(A_B, Tc), scalar1=SCAN_EPS,
                        scalar2=None, op0=Alu.add))
                    sc_val = op("vector",
                                lambda e, cc=cc, n=n, dslot=dslot: e.tensor_tensor_scan(
                                    out=a(A_SS, Tc), data0=a(A_DA + dslot * Tc, Tc),
                                    data1=a(A_B, Tc),
                                    initial=a(A_ST + cc * N + n, 1),
                                    op0=Alu.mult, op1=Alu.add),
                                waits=[(sS, da_val)], inc=sV)
                    da_pool.release(dslot, sV, sc_val)
                    op("vector", lambda e, cc=cc, n=n: e.tensor_copy(
                        out=a(A_ST + cc * N + n, 1), in_=a(A_SS + Tc - 1, 1)),
                        waits=[(sV, sc_val)])
                    if n == 0:
                        last_v = op("vector",
                                    lambda e, cc=cc, rslot=rslot: e.tensor_tensor(
                                        out=a(A_Y + cc * Tc, Tc), in0=a(A_SS, Tc),
                                        in1=bank(REPB[rslot], Tc, Tc), op=Alu.mult),
                                    inc=(sV if cc == MDI - 1 else None))
                    else:
                        op("vector", lambda e, rslot=rslot: e.tensor_tensor(
                            out=a(A_P, Tc), in0=a(A_SS, Tc),
                            in1=bank(REPB[rslot], Tc, Tc), op=Alu.mult))
                        last_v = op("vector",
                                    lambda e, cc=cc: e.scalar_tensor_tensor(
                                        out=a(A_Y + cc * Tc, Tc), in0=a(A_P, Tc),
                                        scalar=1.0, in1=a(A_Y + cc * Tc, Tc),
                                        op0=Alu.mult, op1=Alu.add),
                                    inc=(sV if cc == MDI - 1 else None))
                scan_n_ev[n] = last_v
                rep_pool.release(rslot, sV, last_v)

            ydv = None
            for cc in range(MDI):
                ydv = op("vector", lambda e, cc=cc: e.scalar_tensor_tensor(
                    out=a(A_Y + cc * Tc, Tc), in0=a(A_U + cc * Tc, Tc),
                    scalar=a(A_DV + cc, 1), in1=a(A_Y + cc * Tc, Tc),
                    op0=Alu.mult, op1=Alu.add),
                    inc=(sV if cc == MDI - 1 else None))
            if c == 0:
                tap("y", A_Y, MDI * Tc, (sV, ydv))
            ymw = [(sS, z_evac_all)]
            if c == 0 and dt_tap is not None:
                ymw.append((sTAP, dt_tap))
            ym = None
            for cc in range(MDI):
                ym = op("vector", lambda e, cc=cc: e.scalar_tensor_tensor(
                    out=a(A_DT + cc * Tc, Tc), in0=a(A_Y + cc * Tc, Tc),
                    scalar=1.0, in1=a(A_Z + cc * Tc, Tc),
                    op0=Alu.mult, op1=Alu.mult),
                    waits=(ymw if cc == 0 else []),
                    inc=(sV if cc == MDI - 1 else None))
            ym_ev[c] = ym
            if c == 0:
                tap("ym", A_DT, MDI * Tc, (sV, ym_ev[0]))

            outev = {}
            for m in range(MDM):
                gi, slot, dv = wload(owT, m * MDI)
                pslot, ptok = pp_pool.acquire()
                last = None
                for k in range(MDI):
                    w0 = []
                    if k == 0:
                        w0 = [(sDMA, dv), ptok]
                        if m == 0:
                            w0.append((sV, ym_ev[c]))
                    last = op("tensor",
                              lambda e, k=k, slot=slot, pslot=pslot: e.matmul(
                                  bank(PPB[pslot], 0, Tc), wtile(slot, k),
                                  a(A_DT + k * Tc, Tc),
                                  start=(k == 0), stop=(k == MDI - 1)),
                              waits=w0, inc=(sT if k == MDI - 1 else None))
                ring_last[gi] = last
                out_mm_last[c] = last
                ew = [(sT, last)]
                if m == 0 and c > 0:
                    ew.append((sT, ft_last[c - 1]))
                outev[m] = op("scalar", lambda e, m=m, pslot=pslot: e.activation(
                    out=a(A_AT + m * Tc, Tc), in_=bank(PPB[pslot], 0, Tc),
                    func=Act.Copy), waits=ew, inc=sS)
                pp_pool.release(pslot, sS, outev[m])

            at_tap = (tap("at", A_AT, MDM * Tc, (sS, outev[MDM - 1]))
                      if c == 0 else None)

            relu_all = None
            for m in range(MFF):
                if m % 2 == 0:
                    gi, slot, dv = wload(f1T, m * KDM)
                pslot, ptok = pp_pool.acquire()
                last = None
                for k in range(KDM):
                    w0 = []
                    if k == 0:
                        w0 = [(sDMA, dv), ptok]
                    last = op("tensor",
                              lambda e, m=m, k=k, slot=slot, pslot=pslot: e.matmul(
                                  bank(PPB[pslot], 0, Tc),
                                  wtile(slot, (m % 2) * KDM + k),
                                  a(A_XNF + k * Tc, Tc),
                                  start=(k == 0), stop=(k == KDM - 1)),
                              waits=w0, inc=(sT if k == KDM - 1 else None))
                if m % 2 == 1:
                    ring_last[gi] = last
                ffn1_last[c] = last
                hcol = A_XC + m * XCW if m < MDI else A_U + (m - MDI) * Tc
                ew = [(sT, last)]
                if m == 0:
                    ew.append((sV, ym_ev[c]))
                    if c == 0 and u_tap is not None:
                        ew.append((sTAP, u_tap))
                relu_all = op("scalar",
                              lambda e, m=m, pslot=pslot, hcol=hcol: e.activation(
                                  out=a(hcol, Tc), in_=bank(PPB[pslot], 0, Tc),
                                  func=Act.Relu, bias=a(A_B1 + m, 1)),
                              waits=ew, inc=sS)
                pp_pool.release(pslot, sS, relu_all)

            fe = {}
            for m in range(MDM):
                bslot, btok = bpf_pool.acquire()
                last = None
                for half in range(2):
                    gi, slot, dv = wload(f2T, m * MFF + half * G)
                    for kk in range(G):
                        k = half * G + kk
                        w0 = []
                        if kk == 0:
                            w0 = [(sDMA, dv)]
                            if half == 0:
                                w0.append(btok)
                                if m == 0:
                                    w0.append((sS, relu_all))
                        hcol = (A_XC + k * XCW if k < MDI
                                else A_U + (k - MDI) * Tc)
                        last = op("tensor",
                                  lambda e, kk=kk, k=k, slot=slot, bslot=bslot,
                                  hcol=hcol, half=half: e.matmul(
                                      bank(BPFB[bslot], 0, Tc), wtile(slot, kk),
                                      a(hcol, Tc),
                                      start=(k == 0), stop=(k == 2 * G - 1)),
                                  waits=w0,
                                  inc=(sT if kk == G - 1 else None))
                    ring_last[gi] = last
                ffn2_last[c] = last
                vw = [(sT, last), (sS, outev[m])]
                if c == 0 and m == 0 and at_tap is not None:
                    vw.append((sTAP, at_tap))
                fe[m] = op("vector",
                           lambda e, m=m, bslot=bslot: e.scalar_tensor_tensor(
                               out=a(A_AT + m * Tc, Tc),
                               in0=bank(BPFB[bslot], 0, Tc),
                               scalar=a(A_B2 + m, 1), in1=a(A_AT + m * Tc, Tc),
                               op0=Alu.add, op1=Alu.add), waits=vw, inc=sV)
                bpf_pool.release(bslot, sV, fe[m])

            if c == 0:
                tap("sf", A_AT, MDM * Tc, (sV, fe[MDM - 1]))
                tap("h1", A_XC, MDI * XCW, (sS, relu_all))
                tap("h2", A_U, MDI * Tc, (sS, relu_all))

            for bt in range(4):
                tt, dq = bt // 2, bt % 2
                slot, tok = tp_pool.acquire()
                last = None
                for j in range(4):
                    dc = dq * 4 + j
                    w0 = []
                    if j == 0:
                        w0 = [tok, (sV, fe[dq * 4 + 3])]
                    last = op("tensor",
                              lambda e, tt=tt, dc=dc, j=j, slot=slot: e.matmul(
                                  bank(TPB[slot], j * 128, 128),
                                  a(A_AT + dc * Tc + tt * 128, 128), a(A_ID, 128),
                                  start=(j == 0), stop=(j == 3),
                                  is_transpose=True),
                              waits=w0, inc=(sT if j == 3 else None))
                ft_last[c] = last
                vw = [(sT, last)]
                if bt == 0 and c > 0:
                    vw.append((sST, 16 * (PREP_STORES + c)))
                fev_last[c] = op("vector",
                                 lambda e, tt=tt, dq=dq, slot=slot, sx=sx: e.tensor_add(
                                     out=a(A_OUT + tt * DM + dq * 512, 512),
                                     in0=bank(TPB[slot], 0, 512),
                                     in1=a(sx + tt * DM + dq * 512, 512)),
                                 waits=vw, inc=sV)
                tp_pool.release(slot, sV, fev_last[c])

            store_val[c] = op("gpsimd", lambda e, c=c: e.dma_start(
                out=x_dout_rows(out_d, c),
                in_=a(A_OUT, 2 * DM).rearrange("p (t d) -> p t d", d=DM)),
                waits=[(sV, fev_last[c])], inc=sST, inc_by=16)

            if c + 2 < NCH:
                xload[c + 2] = op("sync", lambda e, c2=c + 2: e.dma_start(
                    out=a(A_SX + (c2 % 2) * 2 * DM, 2 * DM).rearrange(
                        "p (t d) -> p t d", d=DM),
                    in_=x_d[c2 * Tc:(c2 + 1) * Tc, :].rearrange(
                        "(t p) d -> p t d", p=128)),
                    waits=[(sV, fev_last[c])], inc=sDMA, inc_by=16)

        op("gpsimd", None, waits=[(sST, 16 * (PREP_STORES + NCH))])

        def replay(eng):
            items = sch.ops[eng]

            def body(e, items=items):
                for fn, waits, inc, inc_by in items:
                    for s, v in waits:
                        e.wait_ge(s, v)
                    if fn is None:
                        continue
                    ins = fn(e)
                    if inc is not None:
                        ins.then_inc(inc, inc_by)
            return body

        block.sync(replay("sync"))
        block.tensor(replay("tensor"))
        block.vector(replay("vector"))
        block.scalar(replay("scalar"))
        block.gpsimd(replay("gpsimd"))

    return nc


def x_dout_rows(out_d, c):
    return out_d[c * Tc:(c + 1) * Tc, :].rearrange("(t p) d -> p t d", p=128)


_NC_CACHE = {}


def kernel(L0_attn_x, L0_attn_in_proj_w, L0_attn_conv_w, L0_attn_conv_b,
           L0_attn_dt_w, L0_attn_dt_b, L0_attn_A_log, L0_attn_D, L0_attn_B_w,
           L0_attn_C_w, L0_attn_out_w, L0_ffn_W1, L0_ffn_b1, L0_ffn_W2,
           L0_ffn_b2):
    S = L0_attn_x.shape[1]
    if S not in _NC_CACHE:
        _NC_CACHE[S] = build_program(S)
    nc = _NC_CACHE[S]
    w = {
        "in_proj_w": np.ascontiguousarray(L0_attn_in_proj_w, np.float32),
        "conv_w": np.ascontiguousarray(L0_attn_conv_w, np.float32),
        "conv_b": np.ascontiguousarray(L0_attn_conv_b, np.float32),
        "dt_w": np.ascontiguousarray(L0_attn_dt_w, np.float32),
        "dt_b": np.ascontiguousarray(L0_attn_dt_b, np.float32),
        "A_log": np.ascontiguousarray(L0_attn_A_log, np.float32),
        "Dv": np.ascontiguousarray(L0_attn_D, np.float32),
        "B_w": np.ascontiguousarray(L0_attn_B_w, np.float32),
        "C_w": np.ascontiguousarray(L0_attn_C_w, np.float32),
        "out_w": np.ascontiguousarray(L0_attn_out_w, np.float32),
        "W1": np.ascontiguousarray(L0_ffn_W1, np.float32),
        "b1": np.ascontiguousarray(L0_ffn_b1, np.float32),
        "W2": np.ascontiguousarray(L0_ffn_W2, np.float32),
        "b2": np.ascontiguousarray(L0_ffn_b2, np.float32),
    }
    nb = L0_attn_x.shape[0]
    in_maps = [dict(w, x=np.ascontiguousarray(L0_attn_x[b], np.float32))
               for b in range(nb)]
    res = run_bass_kernel_spmd(nc, in_maps, core_ids=list(range(nb)))
    return np.stack([r["out"] for r in res.results], axis=0)


# version 02
# speedup vs baseline: 1.3383x; 1.3383x over previous
import numpy as np
from contextlib import ExitStack

import concourse.bass as bass
import concourse.mybir as mybir
from concourse.bass_utils import run_bass_kernel_spmd

F32 = mybir.dt.float32
Alu = mybir.AluOpType
Act = mybir.ActivationFunctionType
AxX = mybir.AxisListType.X

DM, DI, N, FF, TAPS = 1024, 2048, 16, 4096, 4
Tc = 256
KDM = DM // 128
MDI = DI // 128
MIP = 2 * DI // 128
MFF = FF // 128
MDM = DM // 128
G = 16

TAPC = 0
LN_EPS = 1e-5
FFN_EPS = 1e-5
SCAN_EPS = 1e-6
DT_MIN = 1e-4


class Pool:

    def __init__(self, n):
        self.tok = [None] * n
        self.i = 0
        self.n = n

    def acquire(self):
        s = self.i % self.n
        self.i += 1
        t = self.tok[s]
        self.tok[s] = None
        return s, t

    def release(self, s, sem, val):
        self.tok[s] = (sem, val)


class Sched:
    def __init__(self):
        self.ops = {e: [] for e in ("tensor", "vector", "scalar", "sync", "gpsimd")}
        self.cnt = {}

    def op(self, eng, fn, waits=(), inc=None, inc_by=1):
        w = []
        for t in waits:
            if t is None:
                continue
            s, v = t
            assert v <= self.cnt.get(id(s), 0), f"wait on future event ({eng})"
            if v > 0:
                w.append((s, v))
        val = None
        if inc is not None:
            self.cnt[id(inc)] = self.cnt.get(id(inc), 0) + inc_by
            val = self.cnt[id(inc)]
        self.ops[eng].append((fn, w, inc, inc_by))
        return val


def build_program(S=4096, debug=False):
    NCH = S // Tc
    dbg = {}
    nc = bass.Bass("TRN2", target_bir_lowering=False, debug=False)

    x_d = nc.dram_tensor("x", [S, DM], F32, kind="ExternalInput").ap()
    w_ip = nc.dram_tensor("in_proj_w", [2 * DI, DM], F32, kind="ExternalInput").ap()
    w_cv = nc.dram_tensor("conv_w", [DI, DI, TAPS], F32, kind="ExternalInput").ap()
    b_cv = nc.dram_tensor("conv_b", [DI], F32, kind="ExternalInput").ap()
    w_dt = nc.dram_tensor("dt_w", [DI, DI], F32, kind="ExternalInput").ap()
    b_dt = nc.dram_tensor("dt_b", [DI], F32, kind="ExternalInput").ap()
    a_log = nc.dram_tensor("A_log", [DI, N], F32, kind="ExternalInput").ap()
    d_v = nc.dram_tensor("Dv", [DI], F32, kind="ExternalInput").ap()
    w_b = nc.dram_tensor("B_w", [N, DI], F32, kind="ExternalInput").ap()
    w_c = nc.dram_tensor("C_w", [N, DI], F32, kind="ExternalInput").ap()
    w_ow = nc.dram_tensor("out_w", [DM, DI], F32, kind="ExternalInput").ap()
    w_f1 = nc.dram_tensor("W1", [FF, DM], F32, kind="ExternalInput").ap()
    b_f1 = nc.dram_tensor("b1", [FF], F32, kind="ExternalInput").ap()
    w_f2 = nc.dram_tensor("W2", [DM, FF], F32, kind="ExternalInput").ap()
    b_f2 = nc.dram_tensor("b2", [DM], F32, kind="ExternalInput").ap()
    out_d = nc.dram_tensor("out", [S, DM], F32, kind="ExternalOutput").ap()

    ipT = nc.dram_tensor("ipT", [MIP * KDM * 128, 128], F32).ap()
    cvT = nc.dram_tensor("cvT", [MDI * TAPS * MDI * 128, 128], F32).ap()
    dtT = nc.dram_tensor("dtT", [MDI * MDI * 128, 128], F32).ap()
    owT = nc.dram_tensor("owT", [MDM * MDI * 128, 128], F32).ap()
    f1T = nc.dram_tensor("f1T", [MFF * KDM * 128, 128], F32).ap()
    f2T = nc.dram_tensor("f2T", [MDM * MFF * 128, 128], F32).ap()

    pos = [0]

    def seg(w):
        c = pos[0]
        pos[0] += w
        return c

    A_SX = seg(2 * 2 * DM)
    A_XNT = seg(2 * DM)
    A_OUT = seg(2 * DM)
    A_XNF = seg(2 * KDM * Tc)
    XCW = Tc + 3
    A_XC = seg(MDI * XCW)
    A_Z = seg(MDI * Tc)
    A_U = seg(MDI * Tc)
    A_DT = seg(MDI * Tc)
    A_DTU = seg(MDI * Tc)
    A_Y = seg(MDI * Tc)
    A_AT = seg(MDM * Tc)
    A_WR = seg(2 * G * 128)
    A_DA = seg(2 * Tc)
    A_B = seg(Tc)
    A_SS = seg(Tc)
    A_P = seg(Tc)
    A_ST = seg(Tc)
    A_ASB = seg(Tc)
    A_BCW = seg(2 * MDI * N)
    A_SBC = seg(2 * Tc)
    A_CB = seg(MDI)
    A_DB = seg(MDI)
    A_DV = seg(MDI)
    A_B1 = seg(MFF)
    A_B2 = seg(MDM)
    A_ID = seg(128)
    A_ONE = seg(1)
    A_ONR = seg(128)
    A_STAT = seg(16)
    A_GAM = seg(8)
    A_SEL = seg(N * 128)
    ACOLS = ((pos[0] + 127) // 128) * 128
    assert ACOLS <= 49100, f"arena too big: {ACOLS}"

    P_NAT = A_Z
    P_STR = A_SX
    P_G1 = A_DT
    P_G2 = A_DTU

    sch = Sched()

    with ExitStack() as est:
        en = est.enter_context
        ar_h = en(nc.sbuf_tensor("arena", [128, ACOLS], F32))
        ps_h = en(nc.psum_tensor("ps", [128, 4096], F32))
        sDMA = en(nc.semaphore("sDMA"))
        sST = en(nc.semaphore("sST"))
        sT = en(nc.semaphore("sT"))
        sS = en(nc.semaphore("sS"))
        sV = en(nc.semaphore("sV"))
        sG = en(nc.semaphore("sG"))
        block = en(nc.Block())
        sTAP = en(nc.semaphore("sTAP"))

        def a(c0, w, p0=0, pn=128):
            return ar_h[p0:p0 + pn, c0:c0 + w]

        def bank(k, c0=0, w=512, p0=0, pn=128):
            return ps_h[p0:p0 + pn, k * 512 + c0:k * 512 + c0 + w]

        TPB, PPB, REPB = (0, 1), (4, 5, 6, 7), (2, 3)
        tp_pool, pp_pool, rep_pool = Pool(2), Pool(4), Pool(2)
        da_pool = Pool(2)

        op = sch.op

        def tap(name, c0, w, after, pn=128):
            if not debug:
                return None
            t = nc.dram_tensor("dbg_" + name, [pn, w], F32,
                               kind="ExternalOutput").ap()
            dbg[name] = t
            return op("gpsimd", lambda e, t=t, c0=c0, w=w, pn=pn: e.dma_start(
                out=t, in_=a(c0, w, pn=pn)), waits=[after], inc=sTAP, inc_by=16)

        g_iota = op("gpsimd", lambda e: e.iota(
            out=a(A_ID, 128), pattern=[[-1, 128]], base=0, channel_multiplier=1,
            allow_small_or_imprecise_dtypes=True), inc=sG)
        v_id = op("vector", lambda e: e.tensor_scalar(
            out=a(A_ID, 128), in0=a(A_ID, 128), scalar1=0.0, scalar2=None,
            op0=Alu.is_equal), waits=[(sG, g_iota)], inc=sV)
        g_sel = op("gpsimd", lambda e: e.iota(
            out=a(A_SEL, N * 128, pn=N).rearrange("p (n j) -> p n j", j=128),
            pattern=[[-1, N], [0, 128]], base=0, channel_multiplier=1,
            allow_small_or_imprecise_dtypes=True), inc=sG)
        op("vector", lambda e: e.tensor_scalar(
            out=a(A_SEL, N * 128, pn=N), in0=a(A_SEL, N * 128, pn=N),
            scalar1=0.0, scalar2=None, op0=Alu.is_equal),
            waits=[(sG, g_sel)], inc=sV)
        op("vector", lambda e: e.memset(a(A_ONE, 1), 1.0), inc=sV)
        v_onr = op("vector", lambda e: e.memset(a(A_ONR, 128, pn=1), 1.0), inc=sV)
        op("vector", lambda e: e.memset(a(A_XC, MDI * XCW), 0.0), inc=sV)
        op("vector", lambda e: e.memset(a(A_ST, Tc), 0.0), inc=sV)

        d_last = op("sync", lambda e: e.dma_start(
            out=a(A_CB, MDI), in_=b_cv.rearrange("(c p) -> p c", p=128),
            allow_slow_non_contiguous=True),
            inc=sDMA, inc_by=16)
        d_last = op("sync", lambda e: e.dma_start(
            out=a(A_DB, MDI), in_=b_dt.rearrange("(c p) -> p c", p=128),
            allow_slow_non_contiguous=True),
            inc=sDMA, inc_by=16)
        d_last = op("sync", lambda e: e.dma_start(
            out=a(A_DV, MDI), in_=d_v.rearrange("(c p) -> p c", p=128),
            allow_slow_non_contiguous=True),
            inc=sDMA, inc_by=16)
        d_last = op("sync", lambda e: e.dma_start(
            out=a(A_B1, MFF), in_=b_f1.rearrange("(c p) -> p c", p=128),
            allow_slow_non_contiguous=True),
            inc=sDMA, inc_by=16)
        d_last = op("sync", lambda e: e.dma_start(
            out=a(A_B2, MDM), in_=b_f2.rearrange("(c p) -> p c", p=128),
            allow_slow_non_contiguous=True),
            inc=sDMA, inc_by=16)
        d_alog = op("sync", lambda e: e.dma_start(
            out=a(A_ASB, Tc).rearrange("p (c n) -> p c n", n=N),
            in_=a_log.rearrange("(c p) n -> p c n", p=128)), inc=sDMA, inc_by=16)
        s_exp = op("scalar", lambda e: e.activation(
            out=a(A_ASB, Tc), in_=a(A_ASB, Tc), func=Act.Exp),
            waits=[(sDMA, d_alog)], inc=sS)
        v_asb = op("vector", lambda e: e.tensor_scalar(
            out=a(A_ASB, Tc), in0=a(A_ASB, Tc), scalar1=-1.0, scalar2=None,
            op0=Alu.mult), waits=[(sS, s_exp)], inc=sV)
        tap("asb", A_ASB, Tc, (sV, v_asb))

        nat_tok = [None]
        str_tok = [None]
        qtm_tok = [None]
        n_prep_stores = [0]

        for wi, wsrc in enumerate((w_b, w_c)):
            d0 = op("sync", lambda e, w=wsrc: e.dma_start(
                out=a(P_NAT, DI, pn=N), in_=w), waits=[nat_tok[0]],
                inc=sDMA, inc_by=16)
            for half in range(2):
                slot, tok = tp_pool.acquire()
                last = None
                for j in range(8):
                    k = half * 8 + j
                    last = op("tensor", lambda e, k=k, j=j, slot=slot: e.matmul(
                        bank(TPB[slot], j * N, N),
                        a(P_NAT + k * 128, 128, pn=N),
                        a(A_ID, N, pn=N),
                        start=(j == 0), stop=(j == 7), is_transpose=True),
                        waits=([(sDMA, d0), (sV, v_id), tok] if j == 0 else []),
                        inc=(sT if j == 7 else None))
                ev = op("scalar", lambda e, slot=slot, wi=wi, half=half: e.activation(
                    out=a(A_BCW + wi * MDI * N + half * 8 * N, 8 * N),
                    in_=bank(TPB[slot], 0, 8 * N), func=Act.Copy),
                    waits=[(sT, last)], inc=sS)
                tp_pool.release(slot, sS, ev)
            nat_tok[0] = (sT, last)

        def prep_weight(nat_src_fn, nblocks, ktiles, scratch, width,
                        quant=None, strided_taps=False):
            for m in range(nblocks):
                d0 = op("sync", lambda e, m=m: e.dma_start(
                    out=a(P_NAT, width), in_=nat_src_fn(m)),
                    waits=[nat_tok[0]], inc=sDMA, inc_by=16)
                src_base = P_NAT
                first_w = [(sDMA, d0)]
                if quant is not None:
                    src_base, vq = quant(m, d0)
                    op("tensor", None, waits=[(sV, vq)])
                    first_w = []
                    nat_tok[0] = (sV, vq)
                    qtm_tok[0] = None
                last_ev = None
                nbatch = (ktiles + 3) // 4
                for bt in range(nbatch):
                    slot, tok = tp_pool.acquire()
                    last = None
                    nin = min(4, ktiles - bt * 4)
                    for j in range(nin):
                        k = bt * 4 + j
                        if strided_taps:
                            tap, i = k % TAPS, k // TAPS
                            src = a(src_base, DI * TAPS).rearrange(
                                "p (i t) -> p i t", t=TAPS)[
                                :, i * 128:(i + 1) * 128, tap]
                        else:
                            src = a(src_base + k * 128, 128)
                        w0 = [tok] + (first_w if (bt == 0 and j == 0) else [])
                        last = op("tensor",
                                  lambda e, src=src, slot=slot, j=j, nin=nin: e.matmul(
                                      bank(TPB[slot], j * 128, 128), src,
                                      a(A_ID, 128), start=(j == 0),
                                      stop=(j == nin - 1), is_transpose=True),
                                  waits=(w0 if j == 0 else []),
                                  inc=(sT if j == nin - 1 else None))
                    ev = op("scalar", lambda e, slot=slot, bt=bt, nin=nin: e.activation(
                        out=a(P_STR + bt * 512, nin * 128),
                        in_=bank(TPB[slot], 0, nin * 128), func=Act.Copy),
                        waits=[(sT, last)] + ([str_tok[0]] if bt == 0 else []),
                        inc=sS)
                    tp_pool.release(slot, sS, ev)
                    last_ev = ev
                if quant is None:
                    nat_tok[0] = (sT, last)
                else:
                    qtm_tok[0] = (sT, last)
                stv = op("gpsimd",
                         lambda e, m=m, kt=ktiles, scr=scratch: e.dma_start(
                             out=scr[m * kt * 128:(m + 1) * kt * 128, :].rearrange(
                                 "(t p) c -> p t c", p=128),
                             in_=a(P_STR, kt * 128).rearrange(
                                 "p (t c) -> p t c", c=128)),
                         waits=[(sS, last_ev)], inc=sST, inc_by=16)
                str_tok[0] = (sST, stv)
                n_prep_stores[0] += 1

        def prep_conv():
            for m in range(MDI):
                d0 = op("sync", lambda e, m=m: e.dma_start(
                    out=a(P_NAT, DI * TAPS),
                    in_=w_cv[m * 128:(m + 1) * 128, :, :].rearrange(
                        "p i t -> p (i t)")),
                    waits=[nat_tok[0]], inc=sDMA, inc_by=16)
                last_ev = None
                last = None
                for bt in range(TAPS * MDI // 4):
                    slot, tok = tp_pool.acquire()
                    for j in range(4):
                        k = bt * 4 + j
                        tap, i = k // MDI, k % MDI
                        src = a(P_NAT, DI * TAPS).rearrange(
                            "p (i t) -> p i t", t=TAPS)[
                            :, i * 128:(i + 1) * 128, tap]
                        w0 = [tok] + ([(sDMA, d0)] if (bt == 0 and j == 0) else [])
                        last = op("tensor",
                                  lambda e, src=src, slot=slot, j=j: e.matmul(
                                      bank(TPB[slot], j * 128, 128), src,
                                      a(A_ID, 128), start=(j == 0),
                                      stop=(j == 3), is_transpose=True),
                                  waits=(w0 if j == 0 else []),
                                  inc=(sT if j == 3 else None))
                    ev = op("scalar", lambda e, slot=slot, bt=bt: e.activation(
                        out=a(P_STR + bt * 512, 512),
                        in_=bank(TPB[slot], 0, 512), func=Act.Copy),
                        waits=[(sT, last)] + ([str_tok[0]] if bt == 0 else []),
                        inc=sS)
                    tp_pool.release(slot, sS, ev)
                    last_ev = ev
                nat_tok[0] = (sT, last)
                kt = TAPS * MDI
                stv = op("gpsimd", lambda e, m=m, kt=kt: e.dma_start(
                    out=cvT[m * kt * 128:(m + 1) * kt * 128, :].rearrange(
                        "(t p) c -> p t c", p=128),
                    in_=a(P_STR, kt * 128).rearrange("p (t c) -> p t c", c=128)),
                    waits=[(sS, last_ev)], inc=sST, inc_by=16)
                str_tok[0] = (sST, stv)
                n_prep_stores[0] += 1

        prep_weight(lambda m: w_ip[m * 128:(m + 1) * 128, :], MIP, KDM, ipT, DM)
        prep_conv()
        prep_weight(lambda m: w_dt[m * 128:(m + 1) * 128, :], MDI, MDI, dtT, DI)
        prep_weight(lambda m: w_ow[m * 128:(m + 1) * 128, :], MDM, MDI, owT, DI)

        def gamma_pass(wsrc, nblocks, width, gcol):
            op("vector", lambda e: e.memset(a(gcol, 1), 0.0), inc=sV)
            rv = None
            for m in range(nblocks):
                d0 = op("sync", lambda e, m=m, w=wsrc: e.dma_start(
                    out=a(P_NAT, width), in_=w[m * 128:(m + 1) * 128, :]),
                    waits=[nat_tok[0]], inc=sDMA, inc_by=16)
                gv = op("vector", lambda e, width=width: e.reduce_sum(
                    out=a(gcol + 1, 1), in_=a(P_NAT, width), axis=AxX,
                    apply_absolute_value=True), waits=[(sDMA, d0)], inc=sV)
                rv = op("vector", lambda e: e.tensor_add(
                    out=a(gcol, 1), in0=a(gcol, 1), in1=a(gcol + 1, 1)),
                    waits=[(sV, gv)], inc=sV)
                nat_tok[0] = (sV, rv)
            mm = op("tensor", lambda e: e.matmul(
                bank(PPB[0], 0, 1, pn=1), a(gcol, 1), a(A_ONE, 1),
                start=True, stop=True), waits=[(sV, rv)], inc=sT)
            e1 = op("scalar", lambda e: e.activation(
                out=a(gcol + 2, 1, pn=1), in_=bank(PPB[0], 0, 1, pn=1),
                func=Act.Copy), waits=[(sT, mm)], inc=sS)
            nelem = float(nblocks * 128 * width)
            w1_ = op("vector", lambda e, nelem=nelem: e.tensor_scalar(
                out=a(gcol + 3, 1, pn=1), in0=a(gcol + 2, 1, pn=1),
                scalar1=1.0 / nelem, scalar2=FFN_EPS, op0=Alu.mult, op1=Alu.add),
                waits=[(sS, e1)], inc=sV)
            w2_ = op("vector", lambda e: e.reciprocal(
                out=a(gcol + 4, 1, pn=1), in_=a(gcol + 3, 1, pn=1)),
                waits=[(sV, w1_)], inc=sV)
            w3_ = op("vector", lambda e: e.tensor_mul(
                out=a(gcol + 5, 1, pn=1), in0=a(gcol + 3, 1, pn=1),
                in1=a(gcol + 4, 1, pn=1)), waits=[(sV, w2_)], inc=sV)
            w4_ = op("vector", lambda e: e.tensor_scalar(
                out=a(gcol + 5, 1, pn=1), in0=a(gcol + 5, 1, pn=1),
                scalar1=-1.0, scalar2=2.0, op0=Alu.mult, op1=Alu.add),
                waits=[(sV, w3_)], inc=sV)
            v2 = op("vector", lambda e: e.tensor_mul(
                out=a(gcol + 4, 1, pn=1), in0=a(gcol + 4, 1, pn=1),
                in1=a(gcol + 5, 1, pn=1)), waits=[(sV, w4_)], inc=sV)
            mm2 = op("tensor", lambda e: e.matmul(
                bank(PPB[1], 0, 1), a(A_ONR, 128, pn=1), a(gcol + 4, 1, pn=1),
                start=True, stop=True), waits=[(sV, v2), (sV, v_onr)], inc=sT)
            e2 = op("scalar", lambda e: e.activation(
                out=a(gcol + 6, 1), in_=bank(PPB[1], 0, 1), func=Act.Copy),
                waits=[(sT, mm2)], inc=sS)
            return gcol + 6, e2

        r1c, r1ev = gamma_pass(w_f1, MFF, DM, A_GAM)

        def quant_w1(m, d0):
            op("vector", lambda e: e.tensor_scalar(
                out=a(P_G1, DM), in0=a(P_NAT, DM), scalar1=a(r1c, 1),
                scalar2=0.5, op0=Alu.mult, op1=Alu.is_gt),
                waits=[(sDMA, d0), (sS, r1ev), qtm_tok[0]], inc=sV)
            op("vector", lambda e: e.tensor_scalar(
                out=a(P_G2, DM), in0=a(P_NAT, DM), scalar1=a(r1c, 1),
                scalar2=-0.5, op0=Alu.mult, op1=Alu.is_lt), inc=sV)
            vq = op("vector", lambda e: e.tensor_sub(
                out=a(P_G1, DM), in0=a(P_G1, DM), in1=a(P_G2, DM)), inc=sV)
            return P_G1, vq

        prep_weight(lambda m: w_f1[m * 128:(m + 1) * 128, :], MFF, KDM, f1T, DM,
                    quant=quant_w1)

        r2c, r2ev = gamma_pass(w_f2, MDM, FF, A_GAM)

        def quant_w2(m, d0):
            op("vector", lambda e: e.tensor_scalar(
                out=a(P_STR + 4096, FF), in0=a(P_NAT, FF), scalar1=a(r2c, 1),
                scalar2=0.5, op0=Alu.mult, op1=Alu.is_gt),
                waits=[(sDMA, d0), (sS, r2ev), qtm_tok[0]], inc=sV)
            op("vector", lambda e: e.tensor_scalar(
                out=a(P_NAT + 4096, FF), in0=a(P_NAT, FF), scalar1=a(r2c, 1),
                scalar2=-0.5, op0=Alu.mult, op1=Alu.is_lt), inc=sV)
            vq = op("vector", lambda e: e.tensor_sub(
                out=a(P_STR + 4096, FF), in0=a(P_STR + 4096, FF),
                in1=a(P_NAT + 4096, FF)), inc=sV)
            return P_STR + 4096, vq

        prep_weight(lambda m: w_f2[m * 128:(m + 1) * 128, :], MDM, MFF, f2T, FF,
                    quant=quant_w2)

        PREP_STORES = n_prep_stores[0]
        barrier = (sST, 16 * PREP_STORES)

        conv_last, ffn1_last, ffn2_last = {}, {}, {}
        out_mm_last, ft_last, fev_last, ym_ev, a_ev = {}, {}, {}, {}, {}
        xload, dtp_ev, bc_ev, rep_last, xnf_ready = {}, {}, {}, {}, {}
        tr_last = {}
        xc_all, dt_all, relu_ev, outev_all = {}, {}, {}, {}
        wgrp = [0]
        ring_last = {}

        def wload(scratch, g0, ntiles=G, extra=()):
            gi = wgrp[0]
            wgrp[0] += 1
            slot = gi % 2
            waits = list(extra)
            if gi >= 2:
                waits.append((sT, ring_last[gi - 2]))
            dv = op("sync",
                    lambda e, scr=scratch, g0=g0, nt=ntiles, slot=slot: e.dma_start(
                        out=a(A_WR + slot * G * 128, nt * 128).rearrange(
                            "p (t c) -> p t c", c=128),
                        in_=scr[g0 * 128:(g0 + nt) * 128, :].rearrange(
                            "(t p) c -> p t c", p=128)),
                    waits=waits, inc=sDMA, inc_by=16)
            return gi, slot, dv

        def wtile(slot, i):
            return a(A_WR + slot * G * 128 + i * 128, 128)

        def xnf(c, k):
            return A_XNF + (c % 2) * KDM * Tc + k * Tc

        def emit_xload(c, extra=()):
            xload[c] = op("sync", lambda e, c=c: e.dma_start(
                out=a(A_SX + (c % 2) * 2 * DM, 2 * DM).rearrange(
                    "p (t d) -> p t d", d=DM),
                in_=x_d[c * Tc:(c + 1) * Tc, :].rearrange(
                    "(t p) d -> p t d", p=128)),
                waits=list(extra), inc=sDMA, inc_by=16)

        def emit_ln(c):
            sx = A_SX + (c % 2) * 2 * DM
            hw = [(sDMA, xload[c])]
            if c > 0:
                hw.append((sT, conv_last[c - 1]))
            if c > 1:
                hw.append((sT, ft_last[c - 2]))
            op("vector", lambda e: e.tensor_copy(
                out=a(A_XC, MDI * XCW).rearrange(
                    "p (m w) -> p m w", w=XCW)[:, :, 0:3],
                in_=a(A_XC, MDI * XCW).rearrange(
                    "p (m w) -> p m w", w=XCW)[:, :, Tc:Tc + 3]),
                waits=hw, inc=sV)
            vv = None
            for tt in range(2):
                mv = op("vector", lambda e, tt=tt, sx=sx: e.reduce_sum(
                    out=a(A_STAT + 6 + tt, 1), in_=a(sx + tt * DM, DM), axis=AxX),
                    inc=sV)
                nv = op("vector", lambda e, tt=tt: e.tensor_scalar(
                    out=a(A_STAT + 8 + tt, 1), in0=a(A_STAT + 6 + tt, 1),
                    scalar1=-1.0 / DM, scalar2=None, op0=Alu.mult),
                    waits=[(sV, mv)], inc=sV)
                xw = [(sV, nv)]
                if tt == 0 and c > 0:
                    xw.append((sT, tr_last[c - 1]))
                op("vector", lambda e, tt=tt, sx=sx: e.tensor_scalar(
                    out=a(A_XNT + tt * DM, DM), in0=a(sx + tt * DM, DM),
                    scalar1=a(A_STAT + 8 + tt, 1), scalar2=None, op0=Alu.add),
                    waits=xw, inc=sV)
                op("vector", lambda e, tt=tt: e.tensor_mul(
                    out=a(A_AT + tt * DM, DM), in0=a(A_XNT + tt * DM, DM),
                    in1=a(A_XNT + tt * DM, DM)), inc=sV)
                qv = op("vector", lambda e, tt=tt: e.reduce_sum(
                    out=a(A_STAT + 10 + tt, 1), in_=a(A_AT + tt * DM, DM),
                    axis=AxX), inc=sV)
                vv = op("vector", lambda e, tt=tt: e.tensor_scalar(
                    out=a(A_STAT + tt, 1), in0=a(A_STAT + 10 + tt, 1),
                    scalar1=1.0 / DM, scalar2=LN_EPS, op0=Alu.mult, op1=Alu.add),
                    waits=[(sV, qv)], inc=sV)
            sq = op("scalar", lambda e: e.activation(
                out=a(A_STAT + 2, 2), in_=a(A_STAT, 2), func=Act.Sqrt),
                waits=[(sV, vv)], inc=sS)
            rcv = op("vector", lambda e: e.reciprocal(
                out=a(A_STAT + 4, 2), in_=a(A_STAT + 2, 2)), waits=[(sS, sq)],
                inc=sV)
            vxn = None
            for tt in range(2):
                vxn = op("vector", lambda e, tt=tt: e.tensor_scalar(
                    out=a(A_XNT + tt * DM, DM), in0=a(A_XNT + tt * DM, DM),
                    scalar1=a(A_STAT + 4 + tt, 1), scalar2=None, op0=Alu.mult),
                    waits=([(sV, rcv)] if tt == 0 else []), inc=sV)
            a_ev[c] = vxn
            if c == TAPC:
                tap("xnt", A_XNT, 2 * DM, (sV, a_ev[c]))
                tap("stat", A_STAT, 16, (sV, rcv))

        pend = {}

        def acq(pool, cur_slice):
            s, t = pool.acquire()
            key = (id(pool), s)
            if t is None and key in pend:
                pend.pop(key)[1]()
                t = pool.tok[s]
                pool.tok[s] = None
            return s, t

        def defer(pool, s, sl, fn):
            pend[(id(pool), s)] = (sl, fn)

        def flush_pend(max_slice):
            for key in list(pend):
                if pend[key][0] <= max_slice:
                    pend.pop(key)[1]()

        def build_units(c):
            units = []

            def mk_tp(bt):
                def go(cur_slice=0):
                    tt, dq = bt // 2, bt % 2
                    slot, tok = tp_pool.acquire()
                    last = None
                    for j in range(4):
                        dc = dq * 4 + j
                        last = op("tensor",
                                  lambda e, tt=tt, dc=dc, j=j, slot=slot: e.matmul(
                                      bank(TPB[slot], j * 128, 128),
                                      a(A_XNT + tt * DM + dc * 128, 128),
                                      a(A_ID, 128),
                                      start=(j == 0), stop=(j == 3),
                                      is_transpose=True),
                                  waits=([(sV, a_ev[c]), tok] if j == 0 else []),
                                  inc=(sT if j == 3 else None))
                    tr_last[c] = last
                    ew = [(sT, last)]
                    if bt == 0 and c > 1:
                        ew.append((sT, ffn1_last[c - 2]))
                    ev = op("scalar",
                            lambda e, tt=tt, dq=dq, slot=slot: e.activation(
                                out=a(A_XNF + (c % 2) * KDM * Tc,
                                      KDM * Tc).rearrange(
                                    "p (dc t) -> p dc t", t=Tc)[
                                    :, dq * 4:dq * 4 + 4,
                                    tt * 128:(tt + 1) * 128],
                                in_=bank(TPB[slot], 0, 512).rearrange(
                                    "p (j t) -> p j t", t=128),
                                func=Act.Copy), waits=ew, inc=sS)
                    tp_pool.release(slot, sS, ev)
                    xnf_ready[c] = ev
                    return None
                return go
            for bt in range(4):
                units.append(mk_tp(bt))

            ipst = {}

            def mk_ip(m):
                def go(cur_slice=0):
                    if m % 2 == 0:
                        ipst["g"] = wload(ipT, m * KDM)
                    gi, slot, dv = ipst["g"]
                    pslot, ptok = acq(pp_pool, cur_slice)
                    last = None
                    for k in range(KDM):
                        w0 = []
                        if k == 0:
                            w0 = [(sDMA, dv), ptok, (sS, xnf_ready[c])]
                        last = op("tensor",
                                  lambda e, m=m, k=k, slot=slot,
                                  pslot=pslot: e.matmul(
                                      bank(PPB[pslot], 0, Tc),
                                      wtile(slot, (m % 2) * KDM + k),
                                      a(xnf(c, k), Tc),
                                      start=(k == 0), stop=(k == KDM - 1)),
                                  waits=w0, inc=(sT if k == KDM - 1 else None))
                    if m % 2 == 1:
                        ring_last[gi] = last

                    def evac():
                        ew = [(sT, last), (sV, a_ev[c])]
                        ev = op("scalar", lambda e, m=m, pslot=pslot: e.activation(
                            out=a(A_XC + m * XCW + 3, Tc),
                            in_=bank(PPB[pslot], 0, Tc),
                            func=Act.Copy), waits=ew, inc=sS)
                        pp_pool.release(pslot, sS, ev)
                        xc_all[c] = ev
                    defer(pp_pool, pslot, cur_slice, evac)
                    return None
                return go
            for m in range(MDI):
                units.append(mk_ip(m))

            cvst = {}

            def mk_cv(m, tg):
                def go(cur_slice=0):
                    gi, slot, dv = wload(cvT, (m * TAPS + tg) * MDI)
                    if tg == 0:
                        cvst[m] = acq(pp_pool, cur_slice)
                    pslot, ptok = cvst[m]
                    last = None
                    for i in range(MDI):
                        w0 = []
                        if i == 0:
                            w0 = [(sDMA, dv)]
                            if tg == 0:
                                w0.append(ptok)
                                if m == 0:
                                    w0.append((sS, xc_all[c]))
                        last = op("tensor",
                                  lambda e, tg=tg, i=i, slot=slot,
                                  pslot=pslot: e.matmul(
                                      bank(PPB[pslot], 0, Tc), wtile(slot, i),
                                      a(A_XC + i * XCW + tg, Tc),
                                      start=(tg == 0 and i == 0),
                                      stop=(tg == TAPS - 1 and i == MDI - 1)),
                                  waits=w0, inc=(sT if i == MDI - 1 else None))
                    ring_last[gi] = last
                    if tg < TAPS - 1:
                        return None
                    conv_last[c] = last

                    def evac():
                        ew = [(sT, last)]
                        if m == 0 and c > 0:
                            ew.append((sV, dtp_ev[c - 1]))
                            if debug and c == 1 and utap[0] is not None:
                                ew.append((sTAP, utap[0]))
                        ev = op("scalar", lambda e, m=m, pslot=pslot: e.activation(
                            out=a(A_U + m * Tc, Tc), in_=bank(PPB[pslot], 0, Tc),
                            func=Act.Identity, bias=a(A_CB + m, 1)),
                            waits=ew, inc=sS)
                        pp_pool.release(pslot, sS, ev)
                        ucv_all[c] = ev
                    defer(pp_pool, pslot, cur_slice, evac)
                    return None
                return go
            for m in range(MDI):
                for tg in range(TAPS):
                    units.append(mk_cv(m, tg))
            return units

        ucv_all = {}
        utap = [None]

        def unit_slice(i):
            if i < 4:
                return 0
            if i < 20:
                return 1 + (i - 4) * 3 // 16
            return 4 + (i - 20) * 12 // 64

        def emit_scan(c, units):
            ui = [0]

            def run_units(n):
                while ui[0] < len(units) and unit_slice(ui[0]) <= n:
                    units[ui[0]](n)
                    ui[0] += 1

            for n in range(N):
                rslot, rtok = rep_pool.acquire()
                op("tensor", lambda e, n=n, rslot=rslot: e.matmul(
                    bank(REPB[rslot], 0, Tc), a(A_SEL + n * 128, 128, pn=N),
                    a(A_SBC, Tc, pn=N), start=True, stop=True),
                    waits=[rtok] + ([(sS, bc_ev[c])] if n == 0 else []))
                rep_n = op("tensor", lambda e, n=n, rslot=rslot: e.matmul(
                    bank(REPB[rslot], Tc, Tc), a(A_SEL + n * 128, 128, pn=N),
                    a(A_SBC + Tc, Tc, pn=N), start=False, stop=True),
                    inc=sT)
                rep_last[c] = rep_n
                flush_pend(n - 1)
                run_units(n)
                for cc in range(MDI):
                    dslot, dtok = da_pool.acquire()
                    dw = [dtok]
                    if n == 0 and cc == 0:
                        dw.append((sV, dtp_ev[c]))
                    da_val = op("scalar",
                                lambda e, n=n, cc=cc, dslot=dslot: e.activation(
                                    out=a(A_DA + dslot * Tc, Tc),
                                    in_=a(A_DT + cc * Tc, Tc), func=Act.Exp,
                                    scale=a(A_ASB + cc * N + n, 1)),
                                waits=dw, inc=sS)
                    bw = [(sT, rep_n)] if cc == 0 else []
                    op("vector", lambda e, cc=cc, rslot=rslot: e.tensor_tensor(
                        out=a(A_B, Tc), in0=a(A_DTU + cc * Tc, Tc),
                        in1=bank(REPB[rslot], 0, Tc), op=Alu.mult), waits=bw)
                    op("vector", lambda e: e.tensor_scalar(
                        out=a(A_B, Tc), in0=a(A_B, Tc), scalar1=SCAN_EPS,
                        scalar2=None, op0=Alu.add))
                    sc_val = op("vector",
                                lambda e, cc=cc, n=n, dslot=dslot:
                                e.tensor_tensor_scan(
                                    out=a(A_SS, Tc),
                                    data0=a(A_DA + dslot * Tc, Tc),
                                    data1=a(A_B, Tc),
                                    initial=a(A_ST + cc * N + n, 1),
                                    op0=Alu.mult, op1=Alu.add),
                                waits=[(sS, da_val)], inc=sV)
                    da_pool.release(dslot, sV, sc_val)
                    op("vector", lambda e, cc=cc, n=n: e.tensor_copy(
                        out=a(A_ST + cc * N + n, 1), in_=a(A_SS + Tc - 1, 1)),
                        waits=[(sV, sc_val)])
                    op("vector", lambda e, rslot=rslot: e.tensor_tensor(
                        out=a(A_P, Tc), in0=a(A_SS, Tc),
                        in1=bank(REPB[rslot], Tc, Tc), op=Alu.mult))
                    last_v = op("vector",
                                lambda e, cc=cc: e.scalar_tensor_tensor(
                                    out=a(A_Y + cc * Tc, Tc), in0=a(A_P, Tc),
                                    scalar=1.0, in1=a(A_Y + cc * Tc, Tc),
                                    op0=Alu.mult, op1=Alu.add),
                                inc=(sV if cc == MDI - 1 else None))
                rep_pool.release(rslot, sV, last_v)
            while ui[0] < len(units):
                units[ui[0]](N)
                ui[0] += 1
            flush_pend(99)
            return last_v

        def emit_tail(c):
            ym = None
            for cc in range(MDI):
                ym = op("vector", lambda e, cc=cc: e.scalar_tensor_tensor(
                    out=a(A_Y + cc * Tc, Tc), in0=a(A_Y + cc * Tc, Tc),
                    scalar=1.0, in1=a(A_Z + cc * Tc, Tc),
                    op0=Alu.mult, op1=Alu.mult),
                    inc=(sV if cc == MDI - 1 else None))
            ym_ev[c] = ym
            if c == TAPC:
                ymtap[0] = tap("ym", A_Y, MDI * Tc, (sV, ym_ev[c]))

            outev = {}
            for m in range(MDM):
                gi, slot, dv = wload(owT, m * MDI)
                pslot, ptok = pp_pool.acquire()
                last = None
                for k in range(MDI):
                    w0 = []
                    if k == 0:
                        w0 = [(sDMA, dv), ptok]
                        if m == 0:
                            w0.append((sV, ym_ev[c]))
                    last = op("tensor",
                              lambda e, k=k, slot=slot, pslot=pslot: e.matmul(
                                  bank(PPB[pslot], 0, Tc), wtile(slot, k),
                                  a(A_Y + k * Tc, Tc),
                                  start=(k == 0), stop=(k == MDI - 1)),
                              waits=w0, inc=(sT if k == MDI - 1 else None))
                ring_last[gi] = last
                out_mm_last[c] = last
                ew = [(sT, last)]
                if m == 0:
                    ew.append((sV, ym_ev[c]))
                outev[m] = op("scalar", lambda e, m=m, pslot=pslot: e.activation(
                    out=a(A_AT + m * Tc, Tc), in_=bank(PPB[pslot], 0, Tc),
                    func=Act.Copy), waits=ew, inc=sS)
                pp_pool.release(pslot, sS, outev[m])
            outev_all[c] = outev
            at_tap = (tap("at", A_AT, MDM * Tc, (sS, outev[MDM - 1]))
                      if c == TAPC else None)

            relu_all = None
            for m in range(MFF):
                if m % 2 == 0:
                    gi, slot, dv = wload(f1T, m * KDM)
                pslot, ptok = pp_pool.acquire()
                last = None
                for k in range(KDM):
                    w0 = []
                    if k == 0:
                        w0 = [(sDMA, dv), ptok]
                    last = op("tensor",
                              lambda e, m=m, k=k, slot=slot, pslot=pslot: e.matmul(
                                  bank(PPB[pslot], 0, Tc),
                                  wtile(slot, (m % 2) * KDM + k),
                                  a(xnf(c, k), Tc),
                                  start=(k == 0), stop=(k == KDM - 1)),
                              waits=w0, inc=(sT if k == KDM - 1 else None))
                if m % 2 == 1:
                    ring_last[gi] = last
                ffn1_last[c] = last
                hcol = A_Z + m * Tc if m < MDI else A_DT + (m - MDI) * Tc
                ew = [(sT, last)]
                if m == 0:
                    ew.append((sV, ym_ev[c]))
                    if debug and c == 0 and dttap[0] is not None:
                        ew.append((sTAP, dttap[0]))
                relu_all = op("scalar",
                              lambda e, m=m, pslot=pslot, hcol=hcol: e.activation(
                                  out=a(hcol, Tc), in_=bank(PPB[pslot], 0, Tc),
                                  func=Act.Relu, bias=a(A_B1 + m, 1)),
                              waits=ew, inc=sS)
                pp_pool.release(pslot, sS, relu_all)
            relu_ev[c] = relu_all

            fe = {}
            for m in range(MDM):
                pslot, ptok = pp_pool.acquire()
                last = None
                for half in range(2):
                    gi, slot, dv = wload(f2T, m * MFF + half * G)
                    for kk in range(G):
                        k = half * G + kk
                        w0 = []
                        if kk == 0:
                            w0 = [(sDMA, dv)]
                            if half == 0:
                                w0.append(ptok)
                                if m == 0:
                                    w0.append((sS, relu_all))
                        hcol = (A_Z + k * Tc if k < MDI
                                else A_DT + (k - MDI) * Tc)
                        last = op("tensor",
                                  lambda e, kk=kk, k=k, slot=slot, pslot=pslot,
                                  hcol=hcol: e.matmul(
                                      bank(PPB[pslot], 0, Tc), wtile(slot, kk),
                                      a(hcol, Tc),
                                      start=(k == 0), stop=(k == 2 * G - 1)),
                                  waits=w0, inc=(sT if kk == G - 1 else None))
                    ring_last[gi] = last
                ffn2_last[c] = last
                vw = [(sT, last), (sS, outev[m])]
                if c == TAPC and m == 0 and at_tap is not None:
                    vw.append((sTAP, at_tap))
                fe[m] = op("vector",
                           lambda e, m=m, pslot=pslot: e.scalar_tensor_tensor(
                               out=a(A_AT + m * Tc, Tc),
                               in0=bank(PPB[pslot], 0, Tc),
                               scalar=a(A_B2 + m, 1), in1=a(A_AT + m * Tc, Tc),
                               op0=Alu.add, op1=Alu.add), waits=vw, inc=sV)
                pp_pool.release(pslot, sV, fe[m])
            if c == TAPC:
                tap("sf", A_AT, MDM * Tc, (sV, fe[MDM - 1]))
                tap("h1", A_Z, MDI * Tc, (sS, relu_all))
                tap("h2", A_DT, MDI * Tc, (sS, relu_all))

            sx = A_SX + (c % 2) * 2 * DM
            for bt in range(4):
                tt, dq = bt // 2, bt % 2
                slot, tok = tp_pool.acquire()
                last = None
                for j in range(4):
                    dc = dq * 4 + j
                    w0 = []
                    if j == 0:
                        w0 = [tok, (sV, fe[dq * 4 + 3])]
                    last = op("tensor",
                              lambda e, tt=tt, dc=dc, j=j, slot=slot: e.matmul(
                                  bank(TPB[slot], j * 128, 128),
                                  a(A_AT + dc * Tc + tt * 128, 128),
                                  a(A_ID, 128),
                                  start=(j == 0), stop=(j == 3),
                                  is_transpose=True),
                              waits=w0, inc=(sT if j == 3 else None))
                ft_last[c] = last
                vw = [(sT, last)]
                if bt == 0 and c > 0:
                    vw.append((sST, 16 * (PREP_STORES + c)))
                fev_last[c] = op("vector",
                                 lambda e, tt=tt, dq=dq, slot=slot,
                                 sx=sx: e.tensor_add(
                                     out=a(A_OUT + tt * DM + dq * 512, 512),
                                     in0=bank(TPB[slot], 0, 512),
                                     in1=a(sx + tt * DM + dq * 512, 512)),
                                 waits=vw, inc=sV)
                tp_pool.release(slot, sV, fev_last[c])

            op("gpsimd", lambda e, c=c: e.dma_start(
                out=out_d[c * Tc:(c + 1) * Tc, :].rearrange(
                    "(t p) d -> p t d", p=128),
                in_=a(A_OUT, 2 * DM).rearrange("p (t d) -> p t d", d=DM)),
                waits=[(sV, fev_last[c])], inc=sST, inc_by=16)

        dttap = [None]

        def emit_front_b(c, prolog=False):
            ipz = {}
            z_all = None
            for m in range(MDI, MIP):
                if m % 2 == 0:
                    ipz["g"] = wload(ipT, m * KDM)
                gi, slot, dv = ipz["g"]
                pslot, ptok = pp_pool.acquire()
                last = None
                for k in range(KDM):
                    w0 = []
                    if k == 0:
                        w0 = [(sDMA, dv), ptok]
                        if m == MDI and prolog:
                            w0.append((sS, xnf_ready[c]))
                    last = op("tensor",
                              lambda e, m=m, k=k, slot=slot, pslot=pslot: e.matmul(
                                  bank(PPB[pslot], 0, Tc),
                                  wtile(slot, (m % 2) * KDM + k),
                                  a(xnf(c, k), Tc),
                                  start=(k == 0), stop=(k == KDM - 1)),
                              waits=w0, inc=(sT if k == KDM - 1 else None))
                if m % 2 == 1:
                    ring_last[gi] = last
                ew = [(sT, last)]
                if m == MDI and not prolog:
                    ew.append((sT, ffn2_last[c - 1]))
                z_all = op("scalar", lambda e, m=m, pslot=pslot: e.activation(
                    out=a(A_Z + (m - MDI) * Tc, Tc),
                    in_=bank(PPB[pslot], 0, Tc), func=Act.Silu),
                    waits=ew, inc=sS)
                pp_pool.release(pslot, sS, z_all)

            dt_ev = None
            for m in range(MDI):
                gi, slot, dv = wload(dtT, m * MDI)
                pslot, ptok = pp_pool.acquire()
                last = None
                for k in range(MDI):
                    w0 = []
                    if k == 0:
                        w0 = [(sDMA, dv), ptok]
                        if m == 0:
                            w0.append((sS, ucv_all[c]))
                    last = op("tensor",
                              lambda e, k=k, slot=slot, pslot=pslot: e.matmul(
                                  bank(PPB[pslot], 0, Tc), wtile(slot, k),
                                  a(A_U + k * Tc, Tc),
                                  start=(k == 0), stop=(k == MDI - 1)),
                              waits=w0, inc=(sT if k == MDI - 1 else None))
                ring_last[gi] = last
                ew = [(sT, last)]
                if m == 0 and not prolog:
                    ew.append((sT, ffn2_last[c - 1]))
                dt_ev = op("scalar", lambda e, m=m, pslot=pslot: e.activation(
                    out=a(A_DT + m * Tc, Tc), in_=bank(PPB[pslot], 0, Tc),
                    func=Act.Sigmoid, bias=a(A_DB + m, 1)),
                    waits=ew, inc=sS)
                pp_pool.release(pslot, sS, dt_ev)
            dt_all[c] = dt_ev

            pslot, ptok = pp_pool.acquire()
            last = None
            for half, base in ((0, 0), (1, MDI * N)):
                for k in range(MDI):
                    w0 = []
                    if half == 0 and k == 0:
                        w0 = [ptok]
                    last = op("tensor",
                              lambda e, half=half, base=base, k=k,
                              pslot=pslot: e.matmul(
                                  bank(PPB[pslot], half * Tc, Tc, pn=N),
                                  a(A_BCW + base + k * N, N),
                                  a(A_U + k * Tc, Tc),
                                  start=(half == 0 and k == 0),
                                  stop=(half == 1 and k == MDI - 1)),
                              waits=w0, inc=(sT if (half == 1 and k == MDI - 1)
                                             else None))
            ew = [(sT, last)]
            if not prolog:
                ew.append((sT, rep_last[c - 1]))
            bc_ev[c] = op("scalar", lambda e, pslot=pslot: e.activation(
                out=a(A_SBC, 2 * Tc, pn=N), in_=bank(PPB[pslot], 0, 2 * Tc, pn=N),
                func=Act.Copy), waits=ew, inc=sS)
            pp_pool.release(pslot, sS, bc_ev[c])
            if c == TAPC:
                tap("sbc", A_SBC, 2 * Tc, (sS, bc_ev[c]), pn=N)

            cl = op("vector", lambda e: e.tensor_scalar(
                out=a(A_DT, MDI * Tc), in0=a(A_DT, MDI * Tc), scalar1=DT_MIN,
                scalar2=1.0, op0=Alu.max, op1=Alu.min),
                waits=[(sS, dt_all[c])], inc=sV)
            op("vector", lambda e: e.tensor_mul(
                out=a(A_DTU, MDI * Tc), in0=a(A_DT, MDI * Tc),
                in1=a(A_U, MDI * Tc)), waits=[(sV, cl)], inc=sV)
            yiw = []
            if not prolog:
                yiw.append((sT, out_mm_last[c - 1]))
            if debug and c == 1 and ymtap[0] is not None:
                yiw.append((sTAP, ymtap[0]))
            dtp = None
            for cc in range(MDI):
                dtp = op("vector", lambda e, cc=cc: e.tensor_scalar(
                    out=a(A_Y + cc * Tc, Tc), in0=a(A_U + cc * Tc, Tc),
                    scalar1=a(A_DV + cc, 1), scalar2=None, op0=Alu.mult),
                    waits=(yiw if cc == 0 else []),
                    inc=(sV if cc == MDI - 1 else None))
            dtp_ev[c] = dtp
            if c == TAPC:
                dttap[0] = tap("dt", A_DT, MDI * Tc, (sV, dtp_ev[c]))
                tap("dtu", A_DTU, MDI * Tc, (sV, dtp_ev[c]))

        ymtap = [None]

        emit_xload(0, extra=[barrier])
        if NCH > 1:
            emit_xload(1)
        emit_ln(0)
        for u in build_units(0):
            u(0)
        flush_pend(99)
        if debug and TAPC == 0:
            utap[0] = tap("u", A_U, MDI * Tc, (sS, ucv_all[0]))
            tap("xc", A_XC, MDI * XCW, (sS, xc_all[0]))
            tap("xnf", A_XNF, KDM * Tc, (sS, xnf_ready[0]))
        emit_front_b(0, prolog=True)
        if debug:
            tap("z", A_Z, MDI * Tc, (sS, bc_ev[0]))

        for c in range(NCH):
            if c + 1 < NCH:
                emit_ln(c + 1)
                units = build_units(c + 1)
            else:
                units = []
            sl = emit_scan(c, units)
            if c == TAPC:
                tap("y", A_Y, MDI * Tc, (sV, sl))
            emit_tail(c)
            if c + 1 < NCH:
                emit_front_b(c + 1)
                if debug and c + 1 == TAPC:
                    utap[0] = tap("u", A_U, MDI * Tc, (sS, ucv_all[c + 1]))
                    tap("xc", A_XC, MDI * XCW, (sS, xc_all[c + 1]))
                    tap("xnf", A_XNF + ((c + 1) % 2) * KDM * Tc, KDM * Tc,
                        (sS, xnf_ready[c + 1]))
            if c + 2 < NCH:
                emit_xload(c + 2, extra=[(sV, fev_last[c])])

        op("gpsimd", None, waits=[(sST, 16 * (PREP_STORES + NCH))])

        def replay(eng):
            items = sch.ops[eng]

            def body(e, items=items):
                for fn, waits, inc, inc_by in items:
                    for s, v in waits:
                        e.wait_ge(s, v)
                    if fn is None:
                        continue
                    ins = fn(e)
                    if inc is not None:
                        ins.then_inc(inc, inc_by)
            return body

        block.sync(replay("sync"))
        block.tensor(replay("tensor"))
        block.vector(replay("vector"))
        block.scalar(replay("scalar"))
        block.gpsimd(replay("gpsimd"))

    return nc


def x_dout_rows(out_d, c):
    return out_d[c * Tc:(c + 1) * Tc, :].rearrange("(t p) d -> p t d", p=128)


_NC_CACHE = {}


def kernel(L0_attn_x, L0_attn_in_proj_w, L0_attn_conv_w, L0_attn_conv_b,
           L0_attn_dt_w, L0_attn_dt_b, L0_attn_A_log, L0_attn_D, L0_attn_B_w,
           L0_attn_C_w, L0_attn_out_w, L0_ffn_W1, L0_ffn_b1, L0_ffn_W2,
           L0_ffn_b2):
    S = L0_attn_x.shape[1]
    if S not in _NC_CACHE:
        _NC_CACHE[S] = build_program(S)
    nc = _NC_CACHE[S]
    w = {
        "in_proj_w": np.ascontiguousarray(L0_attn_in_proj_w, np.float32),
        "conv_w": np.ascontiguousarray(L0_attn_conv_w, np.float32),
        "conv_b": np.ascontiguousarray(L0_attn_conv_b, np.float32),
        "dt_w": np.ascontiguousarray(L0_attn_dt_w, np.float32),
        "dt_b": np.ascontiguousarray(L0_attn_dt_b, np.float32),
        "A_log": np.ascontiguousarray(L0_attn_A_log, np.float32),
        "Dv": np.ascontiguousarray(L0_attn_D, np.float32),
        "B_w": np.ascontiguousarray(L0_attn_B_w, np.float32),
        "C_w": np.ascontiguousarray(L0_attn_C_w, np.float32),
        "out_w": np.ascontiguousarray(L0_attn_out_w, np.float32),
        "W1": np.ascontiguousarray(L0_ffn_W1, np.float32),
        "b1": np.ascontiguousarray(L0_ffn_b1, np.float32),
        "W2": np.ascontiguousarray(L0_ffn_W2, np.float32),
        "b2": np.ascontiguousarray(L0_ffn_b2, np.float32),
    }
    nb = L0_attn_x.shape[0]
    in_maps = [dict(w, x=np.ascontiguousarray(L0_attn_x[b], np.float32))
               for b in range(nb)]
    res = run_bass_kernel_spmd(nc, in_maps, core_ids=list(range(nb)))
    return np.stack([r["out"] for r in res.results], axis=0)
